# version 04 (final)
# baseline (speedup 1.0000x reference)
import contextlib
import numpy as np

import concourse.bass as bass
import concourse.mybir as mybir
from concourse.bass_utils import run_bass_kernel_spmd

F32 = mybir.dt.float32
BF16 = mybir.dt.bfloat16
AF = mybir.ActivationFunctionType
ALU = mybir.AluOpType
AXX = mybir.AxisListType.X

D = 2048
S = 4096
T = 256
NMACRO = S // T
DC = D // 128
QKC = 2
NLIN = 32
ICM = 11
ICS = 22
SCALE = 1.0 / 16.0
NEG = -1.0e30

SLICE_TILES = 16
SLOT_COLS = SLICE_TILES * 128
NSLOTS = 2
NSPLIT = 414

ORDER_OC = [0, 1, 2, 3] + [x for j in range(NLIN) for x in (4 + j, 36 + j)]
TILES_PER_MACRO = (68 * 16 + 256 + 4 * (2 * ICM * 32 + 2 * 2 * ICM * 8)
                   + (ICS * 32 + 2 * ICS * 8) + 2 * 256)
SLICES_PER_MACRO = TILES_PER_MACRO // SLICE_TILES
assert TILES_PER_MACRO % SLICE_TILES == 0

_DBG = set()
_NMACRO = NMACRO
_PHASES = 99
_BLIM = 99
_BLV = 1


class _Sem:
    def __init__(self, h):
        self.h = h
        self.n = 0


def build_nc():
    nc = bass.Bass("TRN2", target_bir_lowering=False)

    x = nc.dram_tensor("x", [D, S], F32, kind="ExternalInput").ap()
    w = nc.dram_tensor("w", [128, SLICES_PER_MACRO * SLOT_COLS], F32, kind="ExternalInput").ap()
    gwt = nc.dram_tensor("gwt", [128, 128], F32, kind="ExternalInput").ap()
    gb = nc.dram_tensor("gb", [128, 32], F32, kind="ExternalInput").ap()
    pm = nc.dram_tensor("pm", [128, 1], F32, kind="ExternalInput").ap()
    y = nc.dram_tensor("y", [D, S], F32, kind="ExternalOutput").ap()
    vsp = nc.dram_tensor("vsp", [S, D], F32, kind="Internal").ap()
    _HSPL = NSPLIT // 2
    wsp0 = nc.dram_tensor("wsp0", [128, _HSPL * 2 * SLOT_COLS], BF16, kind="Internal").ap()
    wsp1 = nc.dram_tensor("wsp1", [128, (NSPLIT - _HSPL) * 2 * SLOT_COLS], BF16, kind="Internal").ap()

    def _wsp_slice(sl_in):
        if sl_in < _HSPL:
            c0 = sl_in * 2 * SLOT_COLS
            return wsp0[:, c0:c0 + 2 * SLOT_COLS]
        c0 = (sl_in - _HSPL) * 2 * SLOT_COLS
        return wsp1[:, c0:c0 + 2 * SLOT_COLS]

    ident = nc.inline_tensor(np.eye(128, dtype=np.float32), "ident").ap()
    ones1 = nc.inline_tensor(np.ones((1, 128), dtype=np.float32), "ones1").ap()
    avgv = nc.inline_tensor(np.full((128, 1), 1.0 / D, dtype=np.float32), "avgv").ap()
    qi = np.arange(128, dtype=np.float32)[:, None]
    kj = np.arange(512, dtype=np.float32)[None, :]
    rv = []
    for s_rel in (0.0, -128.0, -256.0, -384.0):
        vv = kj + s_rel - qi
        rv.append(np.where(vv <= 0.0, vv, NEG).astype(np.float32))
    rv.append((kj - qi).astype(np.float32))
    rconst = nc.inline_tensor(np.ascontiguousarray(np.concatenate(rv, axis=1)), "rconst").ap()
    selm = np.zeros((128, 8 * 128), dtype=np.float32)
    for e_ in range(8):
        selm[e_, e_ * 128:(e_ + 1) * 128] = 1.0
    sel8 = nc.inline_tensor(np.ascontiguousarray(selm), "sel8").ap()

    dbg_outs = {}
    dbg_shapes = {"xn": [128, 4096], "h": [128, 4096], "acc0": [128, 4096],
                  "gl": [128, 4096], "comb": [128, 16], "acct": [128, 2048],
                  "q": [128, 512], "k": [128, 8192], "scr": [128, 5632],
                  "accA": [128, 4096], "sp": [128, 1], "afm": [128, 2048]}
    for name in _DBG:
        dbg_outs[name] = nc.dram_tensor("dbg_" + name, dbg_shapes[name], F32,
                                        kind="ExternalOutput").ap()

    with contextlib.ExitStack() as ST:
        E = ST.enter_context
        sxn = E(nc.sbuf_tensor("sxn", [128, DC * T], F32))
        sh_ = E(nc.sbuf_tensor("sh", [128, DC * T], F32))
        sacc = E(nc.sbuf_tensor("sacc", [128, DC * T], F32))
        sgl = E(nc.sbuf_tensor("sgl", [128, 16 * T], F32))
        sK = E(nc.sbuf_tensor("sK", [128, QKC * S], F32))
        sQ = E(nc.sbuf_tensor("sQ", [128, QKC * T], F32))
        svin = E(nc.sbuf_tensor("svin", [128, 2, D], F32))
        scr = E(nc.sbuf_tensor("scr", [128, ICS * T], F32))
        wsl = E(nc.sbuf_tensor("wsl", [128, NSLOTS, SLOT_COLS], F32))
        sB = E(nc.sbuf_tensor("sB", [128, 5 * 512], F32))
        sX = E(nc.sbuf_tensor("sX", [128, 512], F32))
        sP = sX
        sPt = E(nc.sbuf_tensor("sPt", [128, 512], F32))
        sxa = E(nc.sbuf_tensor("sxa", [128, 2, T], F32))
        sxq = E(nc.sbuf_tensor("sxq", [128, 2, T], F32))
        svp = E(nc.sbuf_tensor("svp", [128, 4, 128], F32))
        scb = E(nc.sbuf_tensor("scb", [128, 2, T], F32))
        sgel = E(nc.sbuf_tensor("sgel", [128, T], F32))
        sgeg = E(nc.sbuf_tensor("sgeg", [128, T], F32))
        ssil = E(nc.sbuf_tensor("ssil", [128, T], F32))
        stmp = E(nc.sbuf_tensor("stmp", [128, T], F32))
        stmp2 = E(nc.sbuf_tensor("stmp2", [128, T], F32))
        sident = E(nc.sbuf_tensor("sident", [128, 128], F32))
        sones = E(nc.sbuf_tensor("sones", [1, 128], F32))
        ssel = E(nc.sbuf_tensor("ssel", [128, 8 * 128], F32))
        savg = E(nc.sbuf_tensor("savg", [128, 1], F32))
        sgwt = E(nc.sbuf_tensor("sgwt", [128, 128], F32))
        sgb = E(nc.sbuf_tensor("sgb", [128, 32], F32))
        spm = E(nc.sbuf_tensor("spm", [128, 1], F32))
        ssp = E(nc.sbuf_tensor("ssp", [128, 1], F32))
        seps = E(nc.sbuf_tensor("seps", [128, 1], F32))
        srows = E(nc.sbuf_tensor("srows", [1, 4 * T], F32))
        sge = E(nc.sbuf_tensor("sge", [128, 8], F32))
        sgp = E(nc.sbuf_tensor("sgp", [128, 8], F32))
        sieq = E(nc.sbuf_tensor("sieq", [128, 8], F32))
        smsk = E(nc.sbuf_tensor("smsk", [128, 8], F32))
        sge2 = E(nc.sbuf_tensor("sge2", [128, 8], F32))
        scomb = E(nc.sbuf_tensor("scomb", [128, 2, 8], F32))
        scombT = E(nc.sbuf_tensor("scombT", [128, 256], F32))
        sgm = E(nc.sbuf_tensor("sgm", [128, 1], F32))
        sgm1 = E(nc.sbuf_tensor("sgm1", [128, 1], F32))
        sgm2 = E(nc.sbuf_tensor("sgm2", [128, 1], F32))
        sgs = E(nc.sbuf_tensor("sgs", [128, 1], F32))
        sgr = E(nc.sbuf_tensor("sgr", [128, 1], F32))
        sm_ = E(nc.sbuf_tensor("sm", [128, 1], F32))
        sl_ = E(nc.sbuf_tensor("sl", [128, 1], F32))
        smt = E(nc.sbuf_tensor("smt", [128, 1], F32))
        smn = E(nc.sbuf_tensor("smn", [128, 1], F32))
        sda = E(nc.sbuf_tensor("sda", [128, 1], F32))
        sbv = E(nc.sbuf_tensor("sbv", [128, 1], F32))
        salpha = E(nc.sbuf_tensor("salpha", [128, 1], F32))
        srow = E(nc.sbuf_tensor("srow", [128, 1], F32))
        scs = E(nc.sbuf_tensor("scs", [128, 1], F32))
        slinv = E(nc.sbuf_tensor("slinv", [128, 1], F32))

        xnp = E(nc.sbuf_tensor("xnp", [128, DC, 2 * T], BF16))

        def _sbuf_addr(nm):
            for a_ in nc.m.functions[0].allocations:
                if (hasattr(a_, "memorylocations") and a_.memorylocations
                        and a_.memorylocations[0].name == nm):
                    return a_.memorylocations[0].addr
            raise KeyError(nm)

        wslp = nc.alloc_sbuf_tensor_at("wslp", [128, NSLOTS, 2 * SLOT_COLS], BF16,
                                       offset=_sbuf_addr("wsl"))
        svinp = nc.alloc_sbuf_tensor_at("svinp", [128, 2, 2 * SLOT_COLS], BF16,
                                        offset=_sbuf_addr("svin"))
        glp = nc.alloc_sbuf_tensor_at("glp", [128, 16 * 2 * T], BF16,
                                      offset=_sbuf_addr("sgl"))
        shp = nc.alloc_sbuf_tensor_at("shp", [128, DC * 2 * T], BF16,
                                      offset=_sbuf_addr("sh"))
        scrp = nc.alloc_sbuf_tensor_at("scrp", [128, ICS * 2 * T], BF16,
                                       offset=_sbuf_addr("scr"))

        pb = [E(nc.psum_tensor(f"pb{i}", [128, 512], F32)) for i in range(8)]

        ws = _Sem(E(nc.semaphore("ws")))
        wsl_s = [_Sem(E(nc.semaphore("wsl0"))), _Sem(E(nc.semaphore("wsl1")))]
        xa_s = [_Sem(E(nc.semaphore("xa0"))), _Sem(E(nc.semaphore("xa1")))]
        vi_s = [_Sem(E(nc.semaphore("vi0"))), _Sem(E(nc.semaphore("vi1")))]
        xq = _Sem(E(nc.semaphore("xq")))
        gq = _Sem(E(nc.semaphore("gq")))
        vp_s = [_Sem(E(nc.semaphore(f"vp{i}"))) for i in range(4)]
        ppi = _Sem(E(nc.semaphore("ppi")))
        ppo = _Sem(E(nc.semaphore("ppo")))
        yq = _Sem(E(nc.semaphore("yq")))
        dq = _Sem(E(nc.semaphore("dq")))
        pe = _Sem(E(nc.semaphore("pe")))
        dv = _Sem(E(nc.semaphore("dv")))
        ac = _Sem(E(nc.semaphore("ac")))

        class _ColView:
            def __init__(self, base, off, width):
                self.base, self.off, self.width = base, off, width

            def __getitem__(self, idx):
                p, c = idx
                start = c.start if c.start is not None else 0
                stop = c.stop if c.stop is not None else self.width
                return self.base[p, self.off + start:self.off + stop]

        sacct = _ColView(sgl, 0, D)
        safm = _ColView(sgl, D, D)
        smu = _ColView(srows, 0, T)
        ss2 = _ColView(srows, T, T)
        svar = _ColView(srows, 2 * T, T)
        srsr = _ColView(srows, 3 * T, T)

        block = E(nc.Block())

        OPS = {k: [] for k in ("t", "v", "s", "y", "g")}
        WM = {k: {} for k in OPS}

        def em(eng, fn):
            OPS[eng].append(fn)

        def wt(eng, sem, val):
            if val <= 0 or WM[eng].get(id(sem), -1) >= val:
                return
            WM[eng][id(sem)] = val
            OPS[eng].append(lambda h, s=sem.h, v=val: h.wait_ge(s, v))

        total_slices = SLICES_PER_MACRO * _NMACRO
        release_val = {}
        wstate = {"pos": 0, "next_refill": 0, "last_rel": -1}

        def emit_refill(k):
            if k >= NSLOTS:
                wt("y", pe, release_val[k - NSLOTS])
            sl_in = k % SLICES_PER_MACRO
            slot = k % NSLOTS
            wsl_s[slot].n += 16
            if sl_in < NSPLIT:
                wt("y", ppo, 16 * (sl_in + 1))
                em("y", lambda h, sl=slot, src=_wsp_slice(sl_in): h.dma_start(
                    out=wslp[:, sl, :], in_=src).then_inc(wsl_s[slot].h, 16))
            else:
                c0 = sl_in * SLOT_COLS
                em("y", lambda h, sl=slot, c0=c0: h.dma_start(
                    out=wsl[:, sl, :], in_=w[:, c0:c0 + SLOT_COLS]).then_inc(wsl_s[slot].h, 16))

        def next_wtile():
            t_ = wstate["pos"]
            wstate["pos"] += 1
            sl = t_ // SLICE_TILES
            while wstate["next_refill"] <= min(sl + NSLOTS - 1, total_slices - 1):
                emit_refill(wstate["next_refill"])
                wstate["next_refill"] += 1
            slot = sl % NSLOTS
            wt("t", wsl_s[slot], 16 * (sl // NSLOTS + 1))
            off = (t_ % SLICE_TILES) * 128
            if sl % SLICES_PER_MACRO < NSPLIT:
                return (wslp[:, slot, off:off + 128],
                        wslp[:, slot, SLOT_COLS + off:SLOT_COLS + off + 128])
            return (wsl[:, slot, off:off + 128], None)

        def note_release():
            hi = wstate["pos"] // SLICE_TILES
            for sl in range(wstate["last_rel"] + 1, hi):
                release_val[sl] = pe.n
            wstate["last_rel"] = max(wstate["last_rel"], hi - 1)

        bank_guard = [None] * 8

        def pe_use(b):
            g = bank_guard[b]
            if g is not None:
                wt("t", g[0], g[1])

        def bank_done(b, sem, val):
            bank_guard[b] = (sem, val)

        def t_mm(out_ap, lhs_ap, rhs_ap, start, stop, inc=None):
            if inc is None:
                em("t", lambda h, o=out_ap, a=lhs_ap, r=rhs_ap, s1=start, s2=stop:
                   h.matmul(o, a, r, start=s1, stop=s2))
            else:
                inc.n += 1
                em("t", lambda h, o=out_ap, a=lhs_ap, r=rhs_ap, s1=start, s2=stop, sm=inc.h:
                   h.matmul(o, a, r, start=s1, stop=s2).then_inc(sm, 1))

        def t_tr(out_ap, in_ap, inc=False):
            if inc:
                pe.n += 1
                em("t", lambda h, o=out_ap, i=in_ap: h.transpose(o, i, sident[:, :]).then_inc(pe.h, 1))
            else:
                em("t", lambda h, o=out_ap, i=in_ap: h.transpose(o, i, sident[:, :]))

        def v_op(meth, *aps, inc=False, **kw):
            outs, ins = _classify(meth, aps, kw)
            _autodrain("v", outs, ins)
            if inc:
                dv.n += 1
                em("v", lambda h, m=meth, a=aps, k=kw: getattr(h, m)(*a, **k).then_inc(dv.h, 1))
            else:
                em("v", lambda h, m=meth, a=aps, k=kw: getattr(h, m)(*a, **k))

        _trk = {"v": [set(), set()], "s": [set(), set()]}

        def _apname(a):
            try:
                return a.tensor.name
            except AttributeError:
                return None

        def _autodrain(eng, outs, ins):
            w, r = _trk[eng]
            if (ins & w) or (outs & (w | r)):
                em(eng, lambda h: h.drain())
                w.clear()
                r.clear()
            w |= outs
            r |= ins

        def v_drain():
            em("v", lambda h: h.drain())
            _trk["v"][0].clear()
            _trk["v"][1].clear()

        _IN0 = {"memset": 0}

        def _classify(meth, aps, kw):
            outs = set()
            ins = set()
            nm = [_apname(a) for a in aps]
            if meth == "memset":
                outs |= {nm[0]}
            else:
                outs |= {nm[0]}
                for x in nm[1:]:
                    if x:
                        ins.add(x)
            for k, v in kw.items():
                x = _apname(v)
                if x:
                    (outs if k == "accum_out" else ins).add(x)
            outs.discard(None)
            return outs, ins

        def s_op(meth, *aps, inc=False, **kw):
            outs, ins = _classify(meth, aps, kw)
            _autodrain("s", outs, ins)
            if inc:
                ac.n += 1
                em("s", lambda h, m=meth, a=aps, k=kw: getattr(h, m)(*a, **k).then_inc(ac.h, 1))
            else:
                em("s", lambda h, m=meth, a=aps, k=kw: getattr(h, m)(*a, **k))

        def s_dma(out_ap, in_ap, sem):
            sem.n += 16
            em("s", lambda h, o=out_ap, i=in_ap, sm=sem.h: h.dma_start(out=o, in_=i).then_inc(sm, 16))

        def g_dma(out_ap, in_ap, sem):
            sem.n += 16
            em("g", lambda h, o=out_ap, i=in_ap, sm=sem.h: h.dma_start(out=o, in_=i).then_inc(sm, 16))

        _dbg_done = set()

        def dbg(name, src_ap):
            if name in _DBG and name not in _dbg_done:
                _dbg_done.add(name)
                wt("g", dv, dv.n)
                wt("g", ac, ac.n)
                g_dma(dbg_outs[name][:, :], src_ap, dq)

        prologue_ws = 0
        for dst, src in ((sident[:, :], ident[:, :]), (sones[:, :], ones1[:, :]),
                         (savg[:, :], avgv[:, :]), (sB[:, :], rconst[:, :]),
                         (sgwt[:, :], gwt[:, :]), (sgb[:, :], gb[:, :]),
                         (spm[:, :], pm[:, :]), (ssel[:, :], sel8[:, :])):
            ws.n += 16
            prologue_ws += 16
            em("y", lambda h, o=dst, i=src: h.dma_start(out=o, in_=i).then_inc(ws.h, 16))

        v_op("memset", seps[:, :], 1e-5)
        v_op("memset", scombT[:, :], 0.0, inc=True)
        wt("s", ws, prologue_ws)
        s_op("activation", ssp[:, :], spm[:, :], AF.Exp)
        s_op("activation", ssp[:, :], ssp[:, :], AF.Ln, bias=1.0, inc=True)
        sp_ac = ac.n
        wt("v", ac, sp_ac)
        wt("v", ws, prologue_ws)
        for var in range(5):
            v_op("tensor_scalar", sB[:, var * 512:(var + 1) * 512],
                 sB[:, var * 512:(var + 1) * 512], ssp[:, :], None, ALU.mult,
                 inc=(var == 4))
        dbg("sp", ssp[:, :])

        pp_split_dv = [0]
        for k in range(NSPLIT):
            if k >= 1:
                wt("y", dv, pp_split_dv[0])
            ppi.n += 16
            em("y", lambda h, c0=k * SLOT_COLS: h.dma_start(
                out=svin[:, 0, :], in_=w[:, c0:c0 + SLOT_COLS]).then_inc(ppi.h, 16))
            wt("v", ppi, ppi.n)
            if k >= 1:
                wt("v", ppo, ppo.n)
            v_op("tensor_copy", svinp[:, 1, 0:SLOT_COLS], svin[:, 0, :])
            v_op("tensor_tensor", svinp[:, 1, SLOT_COLS:2 * SLOT_COLS], svin[:, 0, :],
                 svinp[:, 1, 0:SLOT_COLS], ALU.subtract, inc=True)
            pp_split_dv[0] = dv.n
            wt("g", dv, dv.n)
            ppo.n += 16
            em("g", lambda h, dst=_wsp_slice(k): h.dma_start(
                out=dst, in_=svinp[:, 1, :]).then_inc(ppo.h, 16))

        while wstate["next_refill"] < NSLOTS:
            emit_refill(wstate["next_refill"])
            wstate["next_refill"] += 1

        y_guard = 0
        vsp_ready = {}
        xslot_pe = [0, 0]
        vin_rel = [0, 0]
        vpiece_gq = [0, 0, 0, 0]
        vp_evac = [0, 0, 0, 0]
        prev_tr_pe = 0
        prev_resc_dv = 0
        gel_guard = [0]
        sil_guard = [0]

        for m in range(_NMACRO):
            t0 = m * T
            if m == 0:
                wt("t", ws, prologue_ws)
                wt("v", ws, prologue_ws)

            for dc in range(DC):
                slot = dc % 2
                wt("s", pe, xslot_pe[slot])
                s_dma(sxa[:, slot, :], x[dc * 128:(dc + 1) * 128, t0:t0 + T], xa_s[slot])
                x_xq = xa_s[slot].n
                wt("s", xa_s[slot], x_xq)
                s_op("activation", sxq[:, slot, :], sxa[:, slot, :], AF.Square, inc=True)
                sq_ac = ac.n
                if dc == 0:
                    pe_use(6)
                    pe_use(7)
                wt("t", xa_s[slot], x_xq)
                wt("t", ac, sq_ac)
                t_mm(pb[6][0:1, 0:T], savg[:, :], sxa[:, slot, :], dc == 0, dc == DC - 1)
                t_mm(pb[7][0:1, 0:T], savg[:, :], sxq[:, slot, :], dc == 0, dc == DC - 1, inc=pe)
                xslot_pe[slot] = pe.n
            wt("v", pe, pe.n)
            v_op("tensor_copy", smu[0:1, :], pb[6][0:1, 0:T])
            v_op("tensor_copy", ss2[0:1, :], pb[7][0:1, 0:T])
            v_op("tensor_tensor", svar[0:1, :], smu[0:1, :], smu[0:1, :], ALU.mult)
            v_op("tensor_tensor", svar[0:1, :], ss2[0:1, :], svar[0:1, :], ALU.subtract, inc=True)
            var_dv = dv.n
            bank_done(6, dv, var_dv)
            bank_done(7, dv, var_dv)
            wt("s", dv, var_dv)
            s_op("activation", ss2[0:1, :], svar[0:1, :], AF.Sqrt, bias=seps[0:1, :], inc=True)
            wt("v", ac, ac.n)
            v_op("reciprocal", srsr[0:1, :], ss2[0:1, :], inc=True)
            rs_dv = dv.n
            pe_use(4)
            pe_use(5)
            wt("t", dv, rs_dv)
            t_mm(pb[4][:, 0:T], sones[:, :], smu[0:1, 0:T], True, True)
            t_mm(pb[5][:, 0:T], sones[:, :], srsr[0:1, 0:T], True, True, inc=pe)
            bc1_pe = pe.n

            if _PHASES < 2 and m == 0:
                break
            for dc in range(DC):
                slot = dc % 2
                wt("s", pe, xslot_pe[slot])
                s_dma(sxa[:, slot, :], x[dc * 128:(dc + 1) * 128, t0:t0 + T], xa_s[slot])
                x_xq = xa_s[slot].n
                wt("v", xa_s[slot], x_xq)
                wt("v", pe, bc1_pe)
                if dc == 0:
                    wt("v", yq, y_guard)
                v_op("tensor_tensor", stmp[:, :], sxa[:, slot, :], pb[4][:, 0:T], ALU.subtract)
                v_op("tensor_tensor", sxn[:, dc * T:(dc + 1) * T], stmp[:, :], pb[5][:, 0:T], ALU.mult)
                v_op("tensor_tensor", sacc[:, dc * T:(dc + 1) * T], sxa[:, slot, :],
                     sxn[:, dc * T:(dc + 1) * T], ALU.add, inc=True)
                xn_dv = dv.n
                wt("s", dv, xn_dv)
                s_op("activation", sxq[:, slot, :], sxn[:, dc * T:(dc + 1) * T], AF.Square, inc=True)
                sq_ac = ac.n
                if dc == 0:
                    pe_use(6)
                    pe_use(7)
                wt("t", dv, xn_dv)
                wt("t", ac, sq_ac)
                t_mm(pb[6][0:1, 0:T], savg[:, :], sxn[:, dc * T:(dc + 1) * T], dc == 0, dc == DC - 1)
                t_mm(pb[7][0:1, 0:T], savg[:, :], sxq[:, slot, :], dc == 0, dc == DC - 1, inc=pe)
                xslot_pe[slot] = pe.n
            xn_all_dv = dv.n
            dbg("xn", sxn[:, :])
            wt("v", pe, pe.n)
            v_op("tensor_copy", smu[0:1, :], pb[6][0:1, 0:T])
            v_op("tensor_copy", ss2[0:1, :], pb[7][0:1, 0:T])
            v_op("tensor_tensor", svar[0:1, :], smu[0:1, :], smu[0:1, :], ALU.mult)
            v_op("tensor_tensor", svar[0:1, :], ss2[0:1, :], svar[0:1, :], ALU.subtract, inc=True)
            var_dv = dv.n
            bank_done(6, dv, var_dv)
            bank_done(7, dv, var_dv)
            wt("s", dv, var_dv)
            s_op("activation", ss2[0:1, :], svar[0:1, :], AF.Sqrt, bias=seps[0:1, :], inc=True)
            wt("v", ac, ac.n)
            v_op("reciprocal", srsr[0:1, :], ss2[0:1, :], inc=True)
            rs_dv = dv.n
            pe_use(4)
            pe_use(5)
            wt("t", dv, rs_dv)
            t_mm(pb[4][:, 0:T], sones[:, :], smu[0:1, 0:T], True, True)
            t_mm(pb[5][:, 0:T], sones[:, :], srsr[0:1, 0:T], True, True, inc=pe)
            bc2_pe = pe.n

            if _PHASES < 3 and m == 0:
                break
            for dc in range(DC):
                wt("v", pe, bc2_pe)
                v_op("tensor_tensor", stmp[:, :], sxn[:, dc * T:(dc + 1) * T], pb[4][:, 0:T], ALU.subtract)
                v_op("tensor_tensor", stmp2[:, :], stmp[:, :], pb[5][:, 0:T], ALU.mult)
                v_op("tensor_scalar", stmp[:, :], stmp2[:, :],
                     sgb[:, 2 * dc:2 * dc + 1], sgb[:, 2 * dc + 1:2 * dc + 2],
                     ALU.mult, ALU.add)
                v_op("tensor_copy", shp[:, dc * 2 * T:dc * 2 * T + T], stmp[:, :])
                v_op("tensor_tensor", shp[:, dc * 2 * T + T:(dc + 1) * 2 * T],
                     stmp[:, :], shp[:, dc * 2 * T:dc * 2 * T + T], ALU.subtract,
                     inc=True)
            h_dv = dv.n
            bank_done(4, dv, h_dv)
            bank_done(5, dv, h_dv)
            dbg("h", sh_[:, :])
            dbg("acc0", sacc[:, :])

            if _PHASES < 4 and m == 0:
                break
            for th in range(2):
                b = 4 + th
                pe_use(b)
                wt("t", dv, xn_all_dv)
                for dc in range(DC):
                    t_mm(pb[b][:, 0:8],
                         sxn[:, dc * T + th * 128:dc * T + (th + 1) * 128],
                         sgwt[:, dc * 8:dc * 8 + 8], dc == 0, dc == DC - 1,
                         inc=pe if dc == DC - 1 else None)
                wt("v", pe, pe.n)
                v_op("reduce_max", sgm[:, :], pb[b][:, 0:8], AXX)
                v_drain()
                v_op("tensor_scalar", sge[:, :], pb[b][:, 0:8], sgm[:, :], None,
                     ALU.subtract, inc=True)
                bank_done(b, dv, dv.n)
                wt("s", dv, dv.n)
                s_op("activation", sgp[:, :], sge[:, :], AF.Exp, accum_out=sgs[:, :], inc=True)
                wt("v", ac, ac.n)
                v_op("reciprocal", sgr[:, :], sgs[:, :])
                v_drain()
                v_op("tensor_scalar", sgp[:, :], sgp[:, :], sgr[:, :], None, ALU.mult)
                v_drain()
                v_op("reduce_max", sgm1[:, :], sgp[:, :], AXX)
                v_drain()
                v_op("tensor_scalar", sieq[:, :], sgp[:, :], sgm1[:, :], None, ALU.subtract)
                v_drain()
                v_op("tensor_scalar", sieq[:, :], sieq[:, :], 1.0e30, 1.0, ALU.mult, ALU.add)
                v_drain()
                v_op("tensor_scalar", sieq[:, :], sieq[:, :], 1.0, 0.0, ALU.min, ALU.max)
                v_drain()
                v_op("scalar_tensor_tensor", smsk[:, :], sieq[:, :], NEG, sgp[:, :],
                     ALU.mult, ALU.add)
                v_drain()
                v_op("reduce_max", sgm2[:, :], smsk[:, :], AXX)
                v_drain()
                v_op("tensor_scalar", sge2[:, :], sgp[:, :], sgm2[:, :], None, ALU.subtract)
                v_drain()
                v_op("tensor_scalar", sge2[:, :], sge2[:, :], 1.0e30, 1.0, ALU.mult, ALU.add)
                v_drain()
                v_op("tensor_scalar", sge2[:, :], sge2[:, :], 1.0, 0.0, ALU.min, ALU.max)
                v_drain()
                v_op("tensor_tensor", scomb[:, th, :], sgp[:, :], sge2[:, :], ALU.mult, inc=True)
            comb_dv = dv.n
            dbg("comb", scomb[:, :, :].rearrange("p a b -> p (a b)"))
            pe_use(6)
            wt("t", dv, comb_dv)
            for th in range(2):
                t_tr(pb[6][0:8, th * 128:(th + 1) * 128], scomb[:, th, :], inc=(th == 1))
            wt("v", pe, pe.n)
            v_op("tensor_copy", scombT[0:8, 0:256], pb[6][0:8, 0:256], inc=True)
            combT_dv = dv.n
            bank_done(6, dv, combT_dv)

            for cc in range(DC):
                v_op("tensor_copy", xnp[:, cc, 0:T], sxn[:, cc * T:(cc + 1) * T])
                v_op("tensor_tensor", xnp[:, cc, T:2 * T], sxn[:, cc * T:(cc + 1) * T],
                     xnp[:, cc, 0:T], ALU.subtract, inc=(cc == DC - 1))
            xnp_dv = dv.n

            if _PHASES < 5 and m == 0:
                break
            lin_bank = {}
            for idx, oc in enumerate(ORDER_OC):
                b = idx % 4
                pe_use(b)
                wt("t", dv, h_dv)
                for dc in range(DC):
                    whi, wlo = next_wtile()
                    rhi = shp[:, dc * 2 * T:dc * 2 * T + T]
                    rlo = shp[:, dc * 2 * T + T:(dc + 1) * 2 * T]
                    t_mm(pb[b][:, 0:T], whi, rhi, dc == 0, False)
                    t_mm(pb[b][:, 0:T], wlo, rhi, False, False)
                    t_mm(pb[b][:, 0:T], whi, rlo, False, dc == DC - 1,
                         inc=pe if dc == DC - 1 else None)
                note_release()
                grp_pe = pe.n
                if oc <= 1:
                    wt("v", pe, grp_pe)
                    v_op("tensor_copy", sQ[:, oc * T:(oc + 1) * T], pb[b][:, 0:T], inc=True)
                    bank_done(b, dv, dv.n)
                elif oc <= 3:
                    wt("v", pe, grp_pe)
                    v_op("tensor_copy", sK[:, (oc - 2) * S + t0:(oc - 2) * S + t0 + T],
                         pb[b][:, 0:T], inc=True)
                    bank_done(b, dv, dv.n)
                elif oc < 36:
                    lin_bank[oc - 4] = (b, grp_pe)
                else:
                    j = oc - 36
                    lb, lin_pe = lin_bank.pop(j)
                    wt("s", pe, grp_pe)
                    wt("s", dv, gel_guard[0])
                    s_op("activation", sgel[:, :], pb[b][:, 0:T], AF.Gelu, inc=True)
                    gel_ac = ac.n
                    bank_done(b, ac, gel_ac)
                    wt("v", pe, lin_pe)
                    wt("v", ac, gel_ac)
                    if j < 16:
                        v_op("scalar_tensor_tensor", sgeg[:, :],
                             pb[lb][:, 0:T], 1.0, sgel[:, :], ALU.bypass, ALU.mult)
                        v_op("tensor_copy", glp[:, j * 2 * T:j * 2 * T + T], sgeg[:, :])
                        v_op("tensor_tensor", glp[:, j * 2 * T + T:(j + 1) * 2 * T],
                             sgeg[:, :], glp[:, j * 2 * T:j * 2 * T + T], ALU.subtract,
                             inc=True)
                        gel_guard[0] = dv.n
                        bank_done(lb, dv, dv.n)
                    else:
                        v_op("scalar_tensor_tensor", sgeg[:, :], pb[lb][:, 0:T], 1.0,
                             sgel[:, :], ALU.bypass, ALU.mult, inc=True)
                        geg_dv = dv.n
                        gel_guard[0] = dv.n
                        bank_done(lb, dv, geg_dv)
                        for half in range(2):
                            pidx = (2 * (j - 16) + half) % 4
                            pe_use(6)
                            wt("t", dv, vp_evac[pidx])
                            wt("t", dv, geg_dv)
                            t_tr(pb[6][:, pidx * 128:(pidx + 1) * 128],
                                 sgeg[:, half * 128:(half + 1) * 128], inc=True)
                            wt("v", pe, pe.n)
                            wt("v", vp_s[pidx], vpiece_gq[pidx])
                            v_op("tensor_copy", svp[:, pidx, :],
                                 pb[6][:, pidx * 128:(pidx + 1) * 128], inc=True)
                            vp_evac[pidx] = dv.n
                            wt("g", dv, dv.n)
                            g_dma(vsp[t0 + half * 128:t0 + (half + 1) * 128,
                                      (j - 16) * 128:(j - 15) * 128], svp[:, pidx, :], vp_s[pidx])
                            vpiece_gq[pidx] = vp_s[pidx].n
                        if j == 31:
                            vsp_ready[(m, 0)] = tuple(vv.n for vv in vp_s)
                            vsp_ready[(m, 1)] = tuple(vv.n for vv in vp_s)
            bank_done(6, dv, max(vp_evac))
            qk_dv = dv.n
            dbg("q", sQ[:, :])
            dbg("k", sK[:, :])
            dbg("gl", sgl[:, :])

            if _PHASES < 6 and m == 0:
                break
            for d_c in range(DC):
                b = d_c % 2
                pe_use(b)
                wt("t", dv, qk_dv)
                for e_c in range(16):
                    whi, wlo = next_wtile()
                    rhi = glp[:, e_c * 2 * T:e_c * 2 * T + T]
                    rlo = glp[:, e_c * 2 * T + T:(e_c + 1) * 2 * T]
                    t_mm(pb[b][:, 0:T], whi, rhi, e_c == 0, False)
                    t_mm(pb[b][:, 0:T], wlo, rhi, False, False)
                    t_mm(pb[b][:, 0:T], whi, rlo, False, e_c == 15,
                         inc=pe if e_c == 15 else None)
                note_release()
                wt("v", pe, pe.n)
                v_op("tensor_tensor", sacc[:, d_c * T:(d_c + 1) * T],
                     sacc[:, d_c * T:(d_c + 1) * T], pb[b][:, 0:T], ALU.add, inc=True)
                bank_done(b, dv, dv.n)

            if _PHASES < 7 and m == 0:
                break
            for pair in range(4):
                e0 = 2 * pair
                for ei in range(2):
                    pe_use(4 + ei)
                    wt("t", dv, combT_dv)
                    t_mm(pb[4 + ei][:, 0:T], ssel[:, (e0 + ei) * 128:(e0 + ei + 1) * 128],
                         scombT[:, 0:T], True, True, inc=pe)
                    wt("v", pe, pe.n)
                    v_op("tensor_copy", scb[:, ei, :], pb[4 + ei][:, 0:T], inc=True)
                    bank_done(4 + ei, dv, dv.n)
                cb_dv = dv.n
                for ei in range(2):
                    for ic in range(ICM):
                        pe_use(2)
                        pe_use(3)
                        wt("t", dv, xnp_dv)
                        for cc in range(DC):
                            whi, wlo = next_wtile()
                            t_mm(pb[2][:, 0:T], whi, xnp[:, cc, 0:T], cc == 0, False)
                            t_mm(pb[2][:, 0:T], wlo, xnp[:, cc, 0:T], False, False)
                            t_mm(pb[2][:, 0:T], whi, xnp[:, cc, T:2 * T], False,
                                 cc == DC - 1, inc=pe if cc == DC - 1 else None)
                        note_release()
                        h1_pe = pe.n
                        for cc in range(DC):
                            whi, wlo = next_wtile()
                            t_mm(pb[3][:, 0:T], whi, xnp[:, cc, 0:T], cc == 0, False)
                            t_mm(pb[3][:, 0:T], wlo, xnp[:, cc, 0:T], False, False)
                            t_mm(pb[3][:, 0:T], whi, xnp[:, cc, T:2 * T], False,
                                 cc == DC - 1, inc=pe if cc == DC - 1 else None)
                        note_release()
                        hh_pe = pe.n
                        wt("s", pe, h1_pe)
                        wt("s", dv, sil_guard[0])
                        s_op("activation", ssil[:, :], pb[2][:, 0:T], AF.Silu, inc=True)
                        sil_ac = ac.n
                        bank_done(2, ac, sil_ac)
                        wt("v", pe, hh_pe)
                        wt("v", ac, sil_ac)
                        wt("v", dv, cb_dv)
                        v_op("scalar_tensor_tensor", stmp[:, :], pb[3][:, 0:T], 1.0,
                             ssil[:, :], ALU.bypass, ALU.mult)
                        v_op("tensor_tensor", stmp2[:, :],
                             stmp[:, :], scb[:, ei, :], ALU.mult)
                        j2 = (ei * ICM + ic) * 2 * T
                        v_op("tensor_copy", scrp[:, j2:j2 + T], stmp2[:, :])
                        v_op("tensor_tensor", scrp[:, j2 + T:j2 + 2 * T], stmp2[:, :],
                             scrp[:, j2:j2 + T], ALU.subtract, inc=True)
                        sil_guard[0] = dv.n
                        bank_done(3, dv, dv.n)
                hpair_dv = dv.n
                for grp in range(2):
                    for cd in range(8):
                        pe_use(cd)
                    wt("t", dv, hpair_dv)
                    for ei in range(2):
                        for ic in range(ICM):
                            j2 = (ei * ICM + ic) * 2 * T
                            rhi = scrp[:, j2:j2 + T]
                            rlo = scrp[:, j2 + T:j2 + 2 * T]
                            first = ei == 0 and ic == 0
                            last = ei == 1 and ic == ICM - 1
                            for cd in range(8):
                                whi, wlo = next_wtile()
                                t_mm(pb[cd][:, 0:T], whi, rhi, first, False)
                                t_mm(pb[cd][:, 0:T], wlo, rhi, False, False)
                                t_mm(pb[cd][:, 0:T], whi, rlo, False, last,
                                     inc=pe if cd == 7 else None)
                            note_release()
                    wt("v", pe, pe.n)
                    for cd in range(8):
                        d_c = grp * 8 + cd
                        v_op("tensor_tensor", sacc[:, d_c * T:(d_c + 1) * T],
                             sacc[:, d_c * T:(d_c + 1) * T], pb[cd][:, 0:T], ALU.add,
                             inc=(cd == 7))
                    for cd in range(8):
                        bank_done(cd, dv, dv.n)

            if _PHASES < 8 and m == 0:
                break
            for ic in range(ICS):
                pe_use(2)
                pe_use(3)
                wt("t", dv, xnp_dv)
                for cc in range(DC):
                    whi, wlo = next_wtile()
                    t_mm(pb[2][:, 0:T], whi, xnp[:, cc, 0:T], cc == 0, False)
                    t_mm(pb[2][:, 0:T], wlo, xnp[:, cc, 0:T], False, False)
                    t_mm(pb[2][:, 0:T], whi, xnp[:, cc, T:2 * T], False,
                         cc == DC - 1, inc=pe if cc == DC - 1 else None)
                note_release()
                s1_pe = pe.n
                for cc in range(DC):
                    whi, wlo = next_wtile()
                    t_mm(pb[3][:, 0:T], whi, xnp[:, cc, 0:T], cc == 0, False)
                    t_mm(pb[3][:, 0:T], wlo, xnp[:, cc, 0:T], False, False)
                    t_mm(pb[3][:, 0:T], whi, xnp[:, cc, T:2 * T], False,
                         cc == DC - 1, inc=pe if cc == DC - 1 else None)
                note_release()
                ss_pe = pe.n
                wt("s", pe, s1_pe)
                wt("s", dv, sil_guard[0])
                s_op("activation", ssil[:, :], pb[2][:, 0:T], AF.Silu, inc=True)
                sil_ac = ac.n
                bank_done(2, ac, sil_ac)
                wt("v", pe, ss_pe)
                wt("v", ac, sil_ac)
                v_op("scalar_tensor_tensor", stmp2[:, :], pb[3][:, 0:T],
                     1.0, ssil[:, :], ALU.bypass, ALU.mult)
                j2 = ic * 2 * T
                v_op("tensor_copy", scrp[:, j2:j2 + T], stmp2[:, :])
                v_op("tensor_tensor", scrp[:, j2 + T:j2 + 2 * T], stmp2[:, :],
                     scrp[:, j2:j2 + T], ALU.subtract, inc=True)
                sil_guard[0] = dv.n
                bank_done(3, dv, dv.n)
            s_dv = dv.n
            dbg("scr", scr[:, :])
            for grp in range(2):
                for cd in range(8):
                    pe_use(cd)
                wt("t", dv, s_dv)
                for ic in range(ICS):
                    j2 = ic * 2 * T
                    rhi = scrp[:, j2:j2 + T]
                    rlo = scrp[:, j2 + T:j2 + 2 * T]
                    for cd in range(8):
                        whi, wlo = next_wtile()
                        t_mm(pb[cd][:, 0:T], whi, rhi, ic == 0, False)
                        t_mm(pb[cd][:, 0:T], wlo, rhi, False, False)
                        t_mm(pb[cd][:, 0:T], whi, rlo, False, ic == ICS - 1,
                             inc=pe if cd == 7 else None)
                    note_release()
                wt("v", pe, pe.n)
                for cd in range(8):
                    d_c = grp * 8 + cd
                    v_op("tensor_tensor", sacc[:, d_c * T:(d_c + 1) * T],
                         sacc[:, d_c * T:(d_c + 1) * T], pb[cd][:, 0:T], ALU.add,
                         inc=(cd == 7))
                for cd in range(8):
                    bank_done(cd, dv, dv.n)
            dbg("accA", sacc[:, :])

            if _PHASES < 9 and m == 0:
                break
            for qc in range(2):
                qb = t0 + qc * 128
                kend = qb + 128
                v_op("memset", sacct[:, :], 0.0)
                v_op("memset", sl_[:, :], 0.0)
                v_op("memset", sm_[:, :], NEG, inc=True)
                k0 = 0
                while k0 < kend:
                    kw = min(512, kend - k0)
                    s_rel = k0 - qb
                    if s_rel <= -512:
                        var = 4
                        cs_imm = float(s_rel)
                    else:
                        var = {0: 0, -128: 1, -256: 2, -384: 3}[s_rel]
                        cs_imm = 0.0
                    nkc = kw // 128
                    pe_use(4)
                    wt("t", dv, qk_dv)
                    for d in range(QKC):
                        t_mm(pb[4][:, 0:kw],
                             sQ[:, d * T + qc * 128:d * T + (qc + 1) * 128],
                             sK[:, d * S + k0:d * S + k0 + kw],
                             d == 0, d == QKC - 1, inc=pe if d == QKC - 1 else None)
                    sc_pe = pe.n
                    kt_xq = []
                    for kc in range(nkc):
                        slot = ((k0 // 128) + kc) % 2
                        rb = (k0 + kc * 128) // 128
                        src_m = (rb * 128) // T
                        half = ((rb * 128) % T) // 128
                        if _BLV:
                            wt("s", ppo, 16 * NSPLIT)
                            for vi_, vv_ in enumerate(vsp_ready[(src_m, half)]):
                                wt("s", vp_s[vi_], vv_)
                            wt("s", pe, vin_rel[slot])
                            s_dma(svin[:, slot, :], vsp[rb * 128:(rb + 1) * 128, :], vi_s[slot])
                        kt_xq.append((slot, vi_s[slot].n))
                    wt("v", pe, sc_pe)
                    v_op("tensor_scalar", scs[:, :], ssp[:, :], cs_imm, None, ALU.mult)
                    v_op("scalar_tensor_tensor", sX[:, 0:kw], pb[4][:, 0:kw], SCALE,
                         sB[:, var * 512:var * 512 + kw], ALU.mult, ALU.add, inc=True)
                    bank_done(4, dv, dv.n)
                    v_op("reduce_max", smt[:, :], sX[:, 0:kw], AXX)
                    v_drain()
                    v_op("tensor_tensor", smt[:, :], smt[:, :], scs[:, :], ALU.add)
                    v_drain()
                    v_op("tensor_tensor", smn[:, :], sm_[:, :], smt[:, :], ALU.max)
                    v_drain()
                    v_op("tensor_tensor", sda[:, :], sm_[:, :], smn[:, :], ALU.subtract)
                    v_op("tensor_tensor", sbv[:, :], scs[:, :], smn[:, :], ALU.subtract)
                    v_drain()
                    v_op("tensor_copy", sm_[:, :], smn[:, :], inc=True)
                    pre_ac_dv = dv.n
                    if _BLIM < 2:
                        k0 += kw
                        continue
                    wt("s", dv, pre_ac_dv)
                    wt("s", pe, prev_tr_pe)
                    wt("s", dv, prev_resc_dv)
                    s_op("activation", salpha[:, :], sda[:, :], AF.Exp, inc=True)
                    s_op("activation", sP[:, 0:kw], sX[:, 0:kw], AF.Exp, bias=sbv[:, :],
                         accum_out=srow[:, :], inc=True)
                    p_ac = ac.n
                    wt("v", ac, p_ac)
                    v_op("tensor_scalar", sl_[:, :], sl_[:, :], salpha[:, :], srow[:, :],
                         ALU.mult, ALU.add)
                    v_op("tensor_scalar", sacct[:, :], sacct[:, :], salpha[:, :], None,
                         ALU.mult, inc=True)
                    resc_dv = dv.n
                    prev_resc_dv = resc_dv
                    if _BLIM < 3:
                        k0 += kw
                        continue
                    pe_use(5)
                    wt("t", ac, p_ac)
                    for kc in range(nkc):
                        t_tr(pb[5][:, kc * 128:(kc + 1) * 128],
                             sP[:, kc * 128:(kc + 1) * 128], inc=(kc == nkc - 1))
                    prev_tr_pe = pe.n
                    wt("v", pe, prev_tr_pe)
                    v_op("tensor_copy", sPt[:, 0:kw], pb[5][:, 0:kw], inc=True)
                    pt_dv = dv.n
                    bank_done(5, dv, pt_dv)
                    if _BLIM < 4:
                        k0 += kw
                        continue
                    for vd in range(4):
                        pe_use(vd)
                    wt("t", dv, pt_dv)
                    for kc in range(nkc):
                        slot, xv = kt_xq[kc]
                        wt("t", vi_s[slot], xv)
                        for vd in range(4):
                            t_mm(pb[vd][:, 0:512], sPt[:, kc * 128:(kc + 1) * 128],
                                 svin[:, slot, vd * 512:(vd + 1) * 512],
                                 kc == 0, kc == nkc - 1, inc=pe if vd == 3 else None)
                        vin_rel[slot] = pe.n
                    if _BLIM < 5:
                        k0 += kw
                        continue
                    wt("v", pe, pe.n)
                    wt("v", dv, resc_dv)
                    for vd in range(4):
                        v_op("tensor_tensor", sacct[:, vd * 512:(vd + 1) * 512],
                             sacct[:, vd * 512:(vd + 1) * 512], pb[vd][:, 0:512],
                             ALU.add, inc=(vd == 3))
                    for vd in range(4):
                        bank_done(vd, dv, dv.n)
                    k0 += kw
                if _BLIM < 6:
                    continue
                v_drain()
                v_op("reciprocal", slinv[:, :], sl_[:, :])
                v_drain()
                v_op("tensor_scalar", sacct[:, :], sacct[:, :], slinv[:, :], None,
                     ALU.mult, inc=True)
                fin_dv = dv.n
                if qc == 0:
                    dbg("acct", sacct[:, :])
                for vq in range(4):
                    pe_use(6)
                    wt("t", dv, fin_dv)
                    for jj in range(4):
                        t_tr(pb[6][:, jj * 128:(jj + 1) * 128],
                             sacct[:, (vq * 4 + jj) * 128:(vq * 4 + jj + 1) * 128],
                             inc=(jj == 3))
                    wt("v", pe, pe.n)
                    v_op("tensor_copy", safm[:, vq * 512:(vq + 1) * 512],
                         pb[6][:, 0:512], inc=True)
                    bank_done(6, dv, dv.n)
                afm_dv = dv.n
                if qc == 0:
                    dbg("afm", safm[:, :])
                if _BLIM < 7:
                    continue
                for d_c in range(DC):
                    b = d_c % 2
                    pe_use(b)
                    wt("t", dv, afm_dv)
                    for e_c in range(16):
                        ap, _none = next_wtile()
                        t_mm(pb[b][:, 0:128], ap, safm[:, e_c * 128:(e_c + 1) * 128],
                             e_c == 0, e_c == 15, inc=pe if e_c == 15 else None)
                    note_release()
                    wt("v", pe, pe.n)
                    v_op("tensor_tensor",
                         sacc[:, d_c * T + qc * 128:d_c * T + (qc + 1) * 128],
                         sacc[:, d_c * T + qc * 128:d_c * T + (qc + 1) * 128],
                         pb[b][:, 0:128], ALU.add, inc=True)
                    bank_done(b, dv, dv.n)

            wt("g", dv, dv.n)
            yv = y.rearrange("(c p) t -> p c t", p=128)
            g_dma(yv[:, :, t0:t0 + T], sacc[:, :].rearrange("p (c t) -> p c t", c=DC), yq)
            y_guard = yq.n

            em("t", lambda h: h.drain())
            pe.n += 1
            em("t", lambda h: h.nop(nofuse=True).then_inc(pe.h, 1))
            em("v", lambda h: h.drain())
            _trk["v"][0].clear()
            _trk["v"][1].clear()
            dv.n += 1
            em("v", lambda h: h.nop(nofuse=True).then_inc(dv.h, 1))
            wt("s", pe, pe.n)
            wt("s", dv, dv.n)
            em("s", lambda h: h.drain())
            _trk["s"][0].clear()
            _trk["s"][1].clear()
            ac.n += 1
            em("s", lambda h: h.nop(nofuse=True).then_inc(ac.h, 1))
            wt("t", dv, dv.n - 1)
            wt("t", ac, ac.n)
            wt("v", pe, pe.n)
            wt("v", ac, ac.n)

        wt("g", yq, yq.n)
        for vv in vp_s:
            wt("g", vv, vv.n)
        wt("g", dq, dq.n)
        em("g", lambda h: h.nop())

        if _PHASES >= 99 and _BLIM >= 99:
            assert wstate["pos"] == TILES_PER_MACRO * _NMACRO, wstate["pos"]

        @block.tensor
        def _(h):
            for f in OPS["t"]:
                f(h)

        @block.vector
        def _(h):
            for f in OPS["v"]:
                f(h)

        @block.scalar
        def _(h):
            for f in OPS["s"]:
                f(h)

        @block.sync
        def _(h):
            for f in OPS["y"]:
                f(h)

        @block.gpsimd
        def _(h):
            for f in OPS["g"]:
                f(h)

    return nc


def _pack_weights(expand, project, w1, w2, w3, sw1, sw2, sw3):
    tiles = []

    def tl(matT, cc, oc):
        return matT[cc * 128:(cc + 1) * 128, oc * 128:(oc + 1) * 128]

    eT = np.ascontiguousarray(expand.T)
    for oc in ORDER_OC:
        for dc in range(16):
            tiles.append(tl(eT, dc, oc))
    p1T = np.ascontiguousarray(project[:, :2048].T)
    for d_c in range(16):
        for e_c in range(16):
            tiles.append(tl(p1T, e_c, d_c))
    for pair in range(4):
        es = (2 * pair, 2 * pair + 1)
        for e in es:
            w1T = np.ascontiguousarray(w1[e].T)
            w2T = np.ascontiguousarray(w2[e].T)
            for ic in range(ICM):
                for cc in range(16):
                    tiles.append(tl(w1T, cc, ic))
                for cc in range(16):
                    tiles.append(tl(w2T, cc, ic))
        for grp in range(2):
            for e in es:
                w3T = np.ascontiguousarray(w3[e].T)
                for ic in range(ICM):
                    for cd in range(8):
                        tiles.append(tl(w3T, ic, grp * 8 + cd))
    s1T = np.ascontiguousarray(sw1.T)
    s2T = np.ascontiguousarray(sw2.T)
    for ic in range(ICS):
        for cc in range(16):
            tiles.append(tl(s1T, cc, ic))
        for cc in range(16):
            tiles.append(tl(s2T, cc, ic))
    s3T = np.ascontiguousarray(sw3.T)
    for grp in range(2):
        for ic in range(ICS):
            for cd in range(8):
                tiles.append(tl(s3T, ic, grp * 8 + cd))
    p2T = np.ascontiguousarray(project[:, 2048:].T)
    for _qc in range(2):
        for d_c in range(16):
            for e_c in range(16):
                tiles.append(tl(p2T, e_c, d_c))
    assert len(tiles) == TILES_PER_MACRO, len(tiles)
    arr = np.stack(tiles)
    return np.ascontiguousarray(arr.transpose(1, 0, 2).reshape(128, -1))


def kernel(L0_attn_x, L0_attn_ln_gamma, L0_attn_ln_beta, L0_attn_expand,
           L0_attn_project, L0_attn_pos_mult, L0_ffn_gate_w, L0_ffn_w1,
           L0_ffn_w2, L0_ffn_w3, L0_ffn_sw1, L0_ffn_sw2, L0_ffn_sw3):
    xs = np.asarray(L0_attn_x, dtype=np.float32)
    wpack = _pack_weights(np.asarray(L0_attn_expand, np.float32),
                          np.asarray(L0_attn_project, np.float32),
                          np.asarray(L0_ffn_w1, np.float32),
                          np.asarray(L0_ffn_w2, np.float32),
                          np.asarray(L0_ffn_w3, np.float32),
                          np.asarray(L0_ffn_sw1, np.float32),
                          np.asarray(L0_ffn_sw2, np.float32),
                          np.asarray(L0_ffn_sw3, np.float32))
    gate_w = np.asarray(L0_ffn_gate_w, np.float32)
    gwt = np.ascontiguousarray(gate_w.T.reshape(16, 128, 8).transpose(1, 0, 2).reshape(128, 128))
    gamma = np.asarray(L0_attn_ln_gamma, np.float32).reshape(16, 128).T
    beta = np.asarray(L0_attn_ln_beta, np.float32).reshape(16, 128).T
    gbp = np.ascontiguousarray(np.stack([gamma, beta], axis=-1).reshape(128, 32))
    pmv = np.ascontiguousarray(np.broadcast_to(
        np.asarray(L0_attn_pos_mult, np.float32).reshape(()), (128, 1)))

    nc = build_nc()
    in_maps = []
    for i in range(8):
        in_maps.append({
            "x": np.ascontiguousarray(xs[i].T),
            "w": wpack,
            "gwt": gwt,
            "gb": gbp,
            "pm": pmv,
        })
    res = run_bass_kernel_spmd(nc, in_maps, core_ids=list(range(8)))
    out = np.stack([np.asarray(res.results[i]["y"]).T for i in range(8)])
    return np.ascontiguousarray(out.astype(np.float32))


# version 05
# speedup vs baseline: 1.0370x; 1.0370x over previous
import contextlib
import numpy as np

import concourse.bass as bass
import concourse.mybir as mybir
from concourse.bass_utils import run_bass_kernel_spmd

F32 = mybir.dt.float32
BF16 = mybir.dt.bfloat16
AF = mybir.ActivationFunctionType
ALU = mybir.AluOpType
AXX = mybir.AxisListType.X

D = 2048
S = 4096
T = 256
NMACRO = S // T
DC = D // 128
QKC = 2
NLIN = 32
ICM = 11
ICS = 22
SCALE = 1.0 / 16.0
NEG = -1.0e30

SLICE_TILES = 16
SLOT_COLS = SLICE_TILES * 128
NSLOTS = 2
NSPLIT = 414

ORDER_OC = [0, 1, 2, 3] + [x for j in range(NLIN) for x in (4 + j, 36 + j)]
TILES_PER_MACRO = (68 * 16 + 256 + 4 * (2 * ICM * 32 + 2 * 2 * ICM * 8)
                   + (ICS * 32 + 2 * ICS * 8) + 2 * 256)
SLICES_PER_MACRO = TILES_PER_MACRO // SLICE_TILES
assert TILES_PER_MACRO % SLICE_TILES == 0

_DBG = set()
_NMACRO = NMACRO
_PHASES = 99
_BLIM = 99
_BLV = 1


class _Sem:
    def __init__(self, h):
        self.h = h
        self.n = 0


def build_nc():
    nc = bass.Bass("TRN2", target_bir_lowering=False)

    x = nc.dram_tensor("x", [D, S], F32, kind="ExternalInput").ap()
    w = nc.dram_tensor("w", [128, SLICES_PER_MACRO * SLOT_COLS], F32, kind="ExternalInput").ap()
    gwt = nc.dram_tensor("gwt", [128, 128], F32, kind="ExternalInput").ap()
    gb = nc.dram_tensor("gb", [128, 32], F32, kind="ExternalInput").ap()
    pm = nc.dram_tensor("pm", [128, 1], F32, kind="ExternalInput").ap()
    y = nc.dram_tensor("y", [D, S], F32, kind="ExternalOutput").ap()
    vsp = nc.dram_tensor("vsp", [S, D], F32, kind="Internal").ap()
    _HSPL = NSPLIT // 2
    wsp0 = nc.dram_tensor("wsp0", [128, _HSPL * 2 * SLOT_COLS], BF16, kind="Internal").ap()
    wsp1 = nc.dram_tensor("wsp1", [128, (NSPLIT - _HSPL) * 2 * SLOT_COLS], BF16, kind="Internal").ap()

    def _wsp_slice(sl_in):
        if sl_in < _HSPL:
            c0 = sl_in * 2 * SLOT_COLS
            return wsp0[:, c0:c0 + 2 * SLOT_COLS]
        c0 = (sl_in - _HSPL) * 2 * SLOT_COLS
        return wsp1[:, c0:c0 + 2 * SLOT_COLS]

    ident = nc.inline_tensor(np.eye(128, dtype=np.float32), "ident").ap()
    ones1 = nc.inline_tensor(np.ones((1, 128), dtype=np.float32), "ones1").ap()
    avgv = nc.inline_tensor(np.full((128, 1), 1.0 / D, dtype=np.float32), "avgv").ap()
    qi = np.arange(128, dtype=np.float32)[:, None]
    kj = np.arange(512, dtype=np.float32)[None, :]
    rv = []
    for s_rel in (0.0, -128.0, -256.0, -384.0):
        vv = kj + s_rel - qi
        rv.append(np.where(vv <= 0.0, vv, NEG).astype(np.float32))
    rv.append((kj - qi).astype(np.float32))
    rconst = nc.inline_tensor(np.ascontiguousarray(np.concatenate(rv, axis=1)), "rconst").ap()
    selm = np.zeros((128, 8 * 128), dtype=np.float32)
    for e_ in range(8):
        selm[e_, e_ * 128:(e_ + 1) * 128] = 1.0
    sel8 = nc.inline_tensor(np.ascontiguousarray(selm), "sel8").ap()

    dbg_outs = {}
    dbg_shapes = {"xn": [128, 4096], "h": [128, 4096], "acc0": [128, 4096],
                  "gl": [128, 4096], "comb": [128, 16], "acct": [128, 2048],
                  "q": [128, 512], "k": [128, 8192], "scr": [128, 5632],
                  "accA": [128, 4096], "sp": [128, 1], "afm": [128, 2048]}
    for name in _DBG:
        dbg_outs[name] = nc.dram_tensor("dbg_" + name, dbg_shapes[name], F32,
                                        kind="ExternalOutput").ap()

    with contextlib.ExitStack() as ST:
        E = ST.enter_context
        sxn = E(nc.sbuf_tensor("sxn", [128, DC * T], F32))
        sh_ = E(nc.sbuf_tensor("sh", [128, DC * T], F32))
        sacc = E(nc.sbuf_tensor("sacc", [128, DC * T], F32))
        sgl = E(nc.sbuf_tensor("sgl", [128, 16 * T], F32))
        sK = E(nc.sbuf_tensor("sK", [128, QKC * S], F32))
        sQ = E(nc.sbuf_tensor("sQ", [128, QKC * T], F32))
        svin = E(nc.sbuf_tensor("svin", [128, 2, D], F32))
        scr = E(nc.sbuf_tensor("scr", [128, ICS * T], F32))
        wsl = E(nc.sbuf_tensor("wsl", [128, NSLOTS, SLOT_COLS], F32))
        sB = E(nc.sbuf_tensor("sB", [128, 5 * 512], F32))
        sX = E(nc.sbuf_tensor("sX", [128, 512], F32))
        sP = sX
        sPt = E(nc.sbuf_tensor("sPt", [128, 512], F32))
        sxa = E(nc.sbuf_tensor("sxa", [128, 2, T], F32))
        sxq = E(nc.sbuf_tensor("sxq", [128, 2, T], F32))
        svp = E(nc.sbuf_tensor("svp", [128, 4, 128], F32))
        scb = E(nc.sbuf_tensor("scb", [128, 2, T], F32))
        sgel = E(nc.sbuf_tensor("sgel", [128, T], F32))
        sgeg = E(nc.sbuf_tensor("sgeg", [128, T], F32))
        ssil = E(nc.sbuf_tensor("ssil", [128, T], F32))
        stmp = E(nc.sbuf_tensor("stmp", [128, T], F32))
        stmp2 = E(nc.sbuf_tensor("stmp2", [128, T], F32))
        sident = E(nc.sbuf_tensor("sident", [128, 128], F32))
        sones = E(nc.sbuf_tensor("sones", [1, 128], F32))
        ssel = E(nc.sbuf_tensor("ssel", [128, 8 * 128], F32))
        savg = E(nc.sbuf_tensor("savg", [128, 1], F32))
        sgwt = E(nc.sbuf_tensor("sgwt", [128, 128], F32))
        sgb = E(nc.sbuf_tensor("sgb", [128, 32], F32))
        spm = E(nc.sbuf_tensor("spm", [128, 1], F32))
        ssp = E(nc.sbuf_tensor("ssp", [128, 1], F32))
        seps = E(nc.sbuf_tensor("seps", [128, 1], F32))
        srows = E(nc.sbuf_tensor("srows", [1, 4 * T], F32))
        sge = E(nc.sbuf_tensor("sge", [128, 8], F32))
        sgp = E(nc.sbuf_tensor("sgp", [128, 8], F32))
        sieq = E(nc.sbuf_tensor("sieq", [128, 8], F32))
        smsk = E(nc.sbuf_tensor("smsk", [128, 8], F32))
        sge2 = E(nc.sbuf_tensor("sge2", [128, 8], F32))
        scomb = E(nc.sbuf_tensor("scomb", [128, 2, 8], F32))
        scombT = E(nc.sbuf_tensor("scombT", [128, 256], F32))
        sgm = E(nc.sbuf_tensor("sgm", [128, 1], F32))
        sgm1 = E(nc.sbuf_tensor("sgm1", [128, 1], F32))
        sgm2 = E(nc.sbuf_tensor("sgm2", [128, 1], F32))
        sgs = E(nc.sbuf_tensor("sgs", [128, 1], F32))
        sgr = E(nc.sbuf_tensor("sgr", [128, 1], F32))
        sm_ = E(nc.sbuf_tensor("sm", [128, 1], F32))
        sl_ = E(nc.sbuf_tensor("sl", [128, 1], F32))
        smt = E(nc.sbuf_tensor("smt", [128, 1], F32))
        smn = E(nc.sbuf_tensor("smn", [128, 1], F32))
        sda = E(nc.sbuf_tensor("sda", [128, 1], F32))
        sbv = E(nc.sbuf_tensor("sbv", [128, 1], F32))
        salpha = E(nc.sbuf_tensor("salpha", [128, 1], F32))
        srow = E(nc.sbuf_tensor("srow", [128, 1], F32))
        scs = E(nc.sbuf_tensor("scs", [128, 1], F32))
        slinv = E(nc.sbuf_tensor("slinv", [128, 1], F32))

        xnp = E(nc.sbuf_tensor("xnp", [128, DC, 2 * T], BF16))

        def _sbuf_addr(nm):
            for a_ in nc.m.functions[0].allocations:
                if (hasattr(a_, "memorylocations") and a_.memorylocations
                        and a_.memorylocations[0].name == nm):
                    return a_.memorylocations[0].addr
            raise KeyError(nm)

        wslp = nc.alloc_sbuf_tensor_at("wslp", [128, NSLOTS, 2 * SLOT_COLS], BF16,
                                       offset=_sbuf_addr("wsl"))
        svinp = nc.alloc_sbuf_tensor_at("svinp", [128, 2, 2 * SLOT_COLS], BF16,
                                        offset=_sbuf_addr("svin"))
        glp = nc.alloc_sbuf_tensor_at("glp", [128, 16 * 2 * T], BF16,
                                      offset=_sbuf_addr("sgl"))
        shp = nc.alloc_sbuf_tensor_at("shp", [128, DC * 2 * T], BF16,
                                      offset=_sbuf_addr("sh"))
        scrp = nc.alloc_sbuf_tensor_at("scrp", [128, ICS * 2 * T], BF16,
                                       offset=_sbuf_addr("scr"))

        pb = [E(nc.psum_tensor(f"pb{i}", [128, 512], F32)) for i in range(8)]

        ws = _Sem(E(nc.semaphore("ws")))
        wsl_s = [_Sem(E(nc.semaphore("wsl0"))), _Sem(E(nc.semaphore("wsl1")))]
        xa_s = [_Sem(E(nc.semaphore("xa0"))), _Sem(E(nc.semaphore("xa1")))]
        vi_s = [_Sem(E(nc.semaphore("vi0"))), _Sem(E(nc.semaphore("vi1")))]
        xq = _Sem(E(nc.semaphore("xq")))
        gq = _Sem(E(nc.semaphore("gq")))
        vp_s = [_Sem(E(nc.semaphore(f"vp{i}"))) for i in range(4)]
        ppi = _Sem(E(nc.semaphore("ppi")))
        ppo = _Sem(E(nc.semaphore("ppo")))
        yq = _Sem(E(nc.semaphore("yq")))
        dq = _Sem(E(nc.semaphore("dq")))
        pe = _Sem(E(nc.semaphore("pe")))
        dv = _Sem(E(nc.semaphore("dv")))
        ac = _Sem(E(nc.semaphore("ac")))

        class _ColView:
            def __init__(self, base, off, width):
                self.base, self.off, self.width = base, off, width

            def __getitem__(self, idx):
                p, c = idx
                start = c.start if c.start is not None else 0
                stop = c.stop if c.stop is not None else self.width
                return self.base[p, self.off + start:self.off + stop]

        sacct = _ColView(sgl, 0, D)
        safm = _ColView(sgl, D, D)
        smu = _ColView(srows, 0, T)
        ss2 = _ColView(srows, T, T)
        svar = _ColView(srows, 2 * T, T)
        srsr = _ColView(srows, 3 * T, T)

        block = E(nc.Block())

        OPS = {k: [] for k in ("t", "v", "s", "y", "g")}
        WM = {k: {} for k in OPS}

        def em(eng, fn):
            OPS[eng].append(fn)

        def wt(eng, sem, val):
            if val <= 0 or WM[eng].get(id(sem), -1) >= val:
                return
            WM[eng][id(sem)] = val
            OPS[eng].append(lambda h, s=sem.h, v=val: h.wait_ge(s, v))

        total_slices = SLICES_PER_MACRO * _NMACRO
        release_val = {}
        wstate = {"pos": 0, "next_refill": 0, "last_rel": -1}

        def emit_refill(k):
            if k >= NSLOTS:
                wt("y", pe, release_val[k - NSLOTS])
            sl_in = k % SLICES_PER_MACRO
            slot = k % NSLOTS
            wsl_s[slot].n += 16
            if sl_in < NSPLIT:
                wt("y", ppo, 16 * (sl_in + 1))
                em("y", lambda h, sl=slot, src=_wsp_slice(sl_in): h.dma_start(
                    out=wslp[:, sl, :], in_=src).then_inc(wsl_s[slot].h, 16))
            else:
                c0 = sl_in * SLOT_COLS
                em("y", lambda h, sl=slot, c0=c0: h.dma_start(
                    out=wsl[:, sl, :], in_=w[:, c0:c0 + SLOT_COLS]).then_inc(wsl_s[slot].h, 16))

        def next_wtile():
            t_ = wstate["pos"]
            wstate["pos"] += 1
            sl = t_ // SLICE_TILES
            while wstate["next_refill"] <= min(sl + NSLOTS - 1, total_slices - 1):
                emit_refill(wstate["next_refill"])
                wstate["next_refill"] += 1
            slot = sl % NSLOTS
            wt("t", wsl_s[slot], 16 * (sl // NSLOTS + 1))
            off = (t_ % SLICE_TILES) * 128
            if sl % SLICES_PER_MACRO < NSPLIT:
                return (wslp[:, slot, off:off + 128],
                        wslp[:, slot, SLOT_COLS + off:SLOT_COLS + off + 128])
            return (wsl[:, slot, off:off + 128], None)

        def note_release():
            hi = wstate["pos"] // SLICE_TILES
            for sl in range(wstate["last_rel"] + 1, hi):
                release_val[sl] = pe.n
            wstate["last_rel"] = max(wstate["last_rel"], hi - 1)

        bank_guard = [None] * 8

        def pe_use(b):
            g = bank_guard[b]
            if g is not None:
                wt("t", g[0], g[1])

        def bank_done(b, sem, val):
            bank_guard[b] = (sem, val)

        def t_mm(out_ap, lhs_ap, rhs_ap, start, stop, inc=None):
            if inc is None:
                em("t", lambda h, o=out_ap, a=lhs_ap, r=rhs_ap, s1=start, s2=stop:
                   h.matmul(o, a, r, start=s1, stop=s2))
            else:
                inc.n += 1
                em("t", lambda h, o=out_ap, a=lhs_ap, r=rhs_ap, s1=start, s2=stop, sm=inc.h:
                   h.matmul(o, a, r, start=s1, stop=s2).then_inc(sm, 1))

        def t_tr(out_ap, in_ap, inc=False):
            if inc:
                pe.n += 1
                em("t", lambda h, o=out_ap, i=in_ap: h.transpose(o, i, sident[:, :]).then_inc(pe.h, 1))
            else:
                em("t", lambda h, o=out_ap, i=in_ap: h.transpose(o, i, sident[:, :]))

        def v_op(meth, *aps, inc=False, **kw):
            outs, ins = _classify(meth, aps, kw)
            _autodrain("v", outs, ins)
            if inc:
                dv.n += 1
                em("v", lambda h, m=meth, a=aps, k=kw: getattr(h, m)(*a, **k).then_inc(dv.h, 1))
            else:
                em("v", lambda h, m=meth, a=aps, k=kw: getattr(h, m)(*a, **k))

        _trk = {"v": [set(), set()], "s": [set(), set()]}

        def _apname(a):
            try:
                return a.tensor.name
            except AttributeError:
                return None

        def _autodrain(eng, outs, ins):
            w, r = _trk[eng]
            if (ins & w) or (outs & (w | r)):
                em(eng, lambda h: h.drain())
                w.clear()
                r.clear()
            w |= outs
            r |= ins

        def v_drain():
            em("v", lambda h: h.drain())
            _trk["v"][0].clear()
            _trk["v"][1].clear()

        _IN0 = {"memset": 0}

        def _classify(meth, aps, kw):
            outs = set()
            ins = set()
            nm = [_apname(a) for a in aps]
            if meth == "memset":
                outs |= {nm[0]}
            else:
                outs |= {nm[0]}
                for x in nm[1:]:
                    if x:
                        ins.add(x)
            for k, v in kw.items():
                x = _apname(v)
                if x:
                    (outs if k == "accum_out" else ins).add(x)
            outs.discard(None)
            return outs, ins

        def s_op(meth, *aps, inc=False, **kw):
            outs, ins = _classify(meth, aps, kw)
            _autodrain("s", outs, ins)
            if inc:
                ac.n += 1
                em("s", lambda h, m=meth, a=aps, k=kw: getattr(h, m)(*a, **k).then_inc(ac.h, 1))
            else:
                em("s", lambda h, m=meth, a=aps, k=kw: getattr(h, m)(*a, **k))

        def s_dma(out_ap, in_ap, sem):
            sem.n += 16
            em("s", lambda h, o=out_ap, i=in_ap, sm=sem.h: h.dma_start(out=o, in_=i).then_inc(sm, 16))

        def g_dma(out_ap, in_ap, sem):
            sem.n += 16
            em("g", lambda h, o=out_ap, i=in_ap, sm=sem.h: h.dma_start(out=o, in_=i).then_inc(sm, 16))

        _dbg_done = set()

        def dbg(name, src_ap):
            if name in _DBG and name not in _dbg_done:
                _dbg_done.add(name)
                wt("g", dv, dv.n)
                wt("g", ac, ac.n)
                g_dma(dbg_outs[name][:, :], src_ap, dq)

        prologue_ws = 0
        for dst, src in ((sident[:, :], ident[:, :]), (sones[:, :], ones1[:, :]),
                         (savg[:, :], avgv[:, :]), (sB[:, :], rconst[:, :]),
                         (sgwt[:, :], gwt[:, :]), (sgb[:, :], gb[:, :]),
                         (spm[:, :], pm[:, :]), (ssel[:, :], sel8[:, :])):
            ws.n += 16
            prologue_ws += 16
            em("y", lambda h, o=dst, i=src: h.dma_start(out=o, in_=i).then_inc(ws.h, 16))

        v_op("memset", seps[:, :], 1e-5)
        v_op("memset", scombT[:, :], 0.0, inc=True)
        wt("s", ws, prologue_ws)
        s_op("activation", ssp[:, :], spm[:, :], AF.Exp)
        s_op("activation", ssp[:, :], ssp[:, :], AF.Ln, bias=1.0, inc=True)
        sp_ac = ac.n
        wt("v", ac, sp_ac)
        wt("v", ws, prologue_ws)
        for var in range(5):
            v_op("tensor_scalar", sB[:, var * 512:(var + 1) * 512],
                 sB[:, var * 512:(var + 1) * 512], ssp[:, :], None, ALU.mult,
                 inc=(var == 4))
        dbg("sp", ssp[:, :])

        ppg = _Sem(E(nc.semaphore("ppg")))
        for k in range(NSPLIT):
            if k >= 1:
                wt("y", ppg, k)
            ppi.n += 16
            em("y", lambda h, c0=k * SLOT_COLS: h.dma_start(
                out=svin[:, 0, :], in_=w[:, c0:c0 + SLOT_COLS]).then_inc(ppi.h, 16))
            wt("g", ppi, ppi.n)
            if k >= 1:
                wt("g", ppo, ppo.n)
            em("g", lambda h: h.tensor_copy(svinp[:, 1, 0:SLOT_COLS], svin[:, 0, :]))
            em("g", lambda h: h.drain())
            ppg.n += 1
            em("g", lambda h: h.tensor_tensor(
                svinp[:, 1, SLOT_COLS:2 * SLOT_COLS], svin[:, 0, :],
                svinp[:, 1, 0:SLOT_COLS], ALU.subtract).then_inc(ppg.h, 1))
            em("g", lambda h: h.drain())
            ppo.n += 16
            em("g", lambda h, dst=_wsp_slice(k): h.dma_start(
                out=dst, in_=svinp[:, 1, :]).then_inc(ppo.h, 16))

        while wstate["next_refill"] < NSLOTS:
            emit_refill(wstate["next_refill"])
            wstate["next_refill"] += 1

        y_guard = 0
        vsp_ready = {}
        xslot_pe = [0, 0]
        vin_rel = [0, 0]
        vpiece_gq = [0, 0, 0, 0]
        vp_evac = [0, 0, 0, 0]
        prev_tr_pe = 0
        prev_resc_dv = 0
        gel_guard = [0]
        sil_guard = [0]

        for m in range(_NMACRO):
            t0 = m * T
            if m == 0:
                wt("t", ws, prologue_ws)
                wt("v", ws, prologue_ws)

            for dc in range(DC):
                slot = dc % 2
                wt("s", pe, xslot_pe[slot])
                s_dma(sxa[:, slot, :], x[dc * 128:(dc + 1) * 128, t0:t0 + T], xa_s[slot])
                x_xq = xa_s[slot].n
                wt("s", xa_s[slot], x_xq)
                s_op("activation", sxq[:, slot, :], sxa[:, slot, :], AF.Square, inc=True)
                sq_ac = ac.n
                if dc == 0:
                    pe_use(6)
                    pe_use(7)
                wt("t", xa_s[slot], x_xq)
                wt("t", ac, sq_ac)
                t_mm(pb[6][0:1, 0:T], savg[:, :], sxa[:, slot, :], dc == 0, dc == DC - 1)
                t_mm(pb[7][0:1, 0:T], savg[:, :], sxq[:, slot, :], dc == 0, dc == DC - 1, inc=pe)
                xslot_pe[slot] = pe.n
            wt("v", pe, pe.n)
            v_op("tensor_copy", smu[0:1, :], pb[6][0:1, 0:T])
            v_op("tensor_copy", ss2[0:1, :], pb[7][0:1, 0:T])
            v_op("tensor_tensor", svar[0:1, :], smu[0:1, :], smu[0:1, :], ALU.mult)
            v_op("tensor_tensor", svar[0:1, :], ss2[0:1, :], svar[0:1, :], ALU.subtract, inc=True)
            var_dv = dv.n
            bank_done(6, dv, var_dv)
            bank_done(7, dv, var_dv)
            wt("s", dv, var_dv)
            s_op("activation", ss2[0:1, :], svar[0:1, :], AF.Sqrt, bias=seps[0:1, :], inc=True)
            wt("v", ac, ac.n)
            v_op("reciprocal", srsr[0:1, :], ss2[0:1, :], inc=True)
            rs_dv = dv.n
            pe_use(4)
            pe_use(5)
            wt("t", dv, rs_dv)
            t_mm(pb[4][:, 0:T], sones[:, :], smu[0:1, 0:T], True, True)
            t_mm(pb[5][:, 0:T], sones[:, :], srsr[0:1, 0:T], True, True, inc=pe)
            bc1_pe = pe.n

            if _PHASES < 2 and m == 0:
                break
            for dc in range(DC):
                slot = dc % 2
                wt("s", pe, xslot_pe[slot])
                s_dma(sxa[:, slot, :], x[dc * 128:(dc + 1) * 128, t0:t0 + T], xa_s[slot])
                x_xq = xa_s[slot].n
                wt("v", xa_s[slot], x_xq)
                wt("v", pe, bc1_pe)
                if dc == 0:
                    wt("v", yq, y_guard)
                v_op("tensor_tensor", stmp[:, :], sxa[:, slot, :], pb[4][:, 0:T], ALU.subtract)
                v_op("tensor_tensor", sxn[:, dc * T:(dc + 1) * T], stmp[:, :], pb[5][:, 0:T], ALU.mult)
                v_op("tensor_tensor", sacc[:, dc * T:(dc + 1) * T], sxa[:, slot, :],
                     sxn[:, dc * T:(dc + 1) * T], ALU.add, inc=True)
                xn_dv = dv.n
                wt("s", dv, xn_dv)
                s_op("activation", sxq[:, slot, :], sxn[:, dc * T:(dc + 1) * T], AF.Square, inc=True)
                sq_ac = ac.n
                if dc == 0:
                    pe_use(6)
                    pe_use(7)
                wt("t", dv, xn_dv)
                wt("t", ac, sq_ac)
                t_mm(pb[6][0:1, 0:T], savg[:, :], sxn[:, dc * T:(dc + 1) * T], dc == 0, dc == DC - 1)
                t_mm(pb[7][0:1, 0:T], savg[:, :], sxq[:, slot, :], dc == 0, dc == DC - 1, inc=pe)
                xslot_pe[slot] = pe.n
            xn_all_dv = dv.n
            dbg("xn", sxn[:, :])
            wt("v", pe, pe.n)
            v_op("tensor_copy", smu[0:1, :], pb[6][0:1, 0:T])
            v_op("tensor_copy", ss2[0:1, :], pb[7][0:1, 0:T])
            v_op("tensor_tensor", svar[0:1, :], smu[0:1, :], smu[0:1, :], ALU.mult)
            v_op("tensor_tensor", svar[0:1, :], ss2[0:1, :], svar[0:1, :], ALU.subtract, inc=True)
            var_dv = dv.n
            bank_done(6, dv, var_dv)
            bank_done(7, dv, var_dv)
            wt("s", dv, var_dv)
            s_op("activation", ss2[0:1, :], svar[0:1, :], AF.Sqrt, bias=seps[0:1, :], inc=True)
            wt("v", ac, ac.n)
            v_op("reciprocal", srsr[0:1, :], ss2[0:1, :], inc=True)
            rs_dv = dv.n
            pe_use(4)
            pe_use(5)
            wt("t", dv, rs_dv)
            t_mm(pb[4][:, 0:T], sones[:, :], smu[0:1, 0:T], True, True)
            t_mm(pb[5][:, 0:T], sones[:, :], srsr[0:1, 0:T], True, True, inc=pe)
            bc2_pe = pe.n

            if _PHASES < 3 and m == 0:
                break
            for th in range(2):
                b = 6 + th
                pe_use(b)
                wt("t", dv, xn_all_dv)
                for dc in range(DC):
                    t_mm(pb[b][:, 0:8],
                         sxn[:, dc * T + th * 128:dc * T + (th + 1) * 128],
                         sgwt[:, dc * 8:dc * 8 + 8], dc == 0, dc == DC - 1,
                         inc=pe if dc == DC - 1 else None)
                wt("v", pe, pe.n)
                v_op("reduce_max", sgm[:, :], pb[b][:, 0:8], AXX)
                v_drain()
                v_op("tensor_scalar", sge[:, :], pb[b][:, 0:8], sgm[:, :], None,
                     ALU.subtract, inc=True)
                bank_done(b, dv, dv.n)
                wt("s", dv, dv.n)
                s_op("activation", sgp[:, :], sge[:, :], AF.Exp, accum_out=sgs[:, :], inc=True)
                wt("v", ac, ac.n)
                v_op("reciprocal", sgr[:, :], sgs[:, :])
                v_drain()
                v_op("tensor_scalar", sgp[:, :], sgp[:, :], sgr[:, :], None, ALU.mult)
                v_drain()
                v_op("reduce_max", sgm1[:, :], sgp[:, :], AXX)
                v_drain()
                v_op("tensor_scalar", sieq[:, :], sgp[:, :], sgm1[:, :], None, ALU.subtract)
                v_drain()
                v_op("tensor_scalar", sieq[:, :], sieq[:, :], 1.0e30, 1.0, ALU.mult, ALU.add)
                v_drain()
                v_op("tensor_scalar", sieq[:, :], sieq[:, :], 1.0, 0.0, ALU.min, ALU.max)
                v_drain()
                v_op("scalar_tensor_tensor", smsk[:, :], sieq[:, :], NEG, sgp[:, :],
                     ALU.mult, ALU.add)
                v_drain()
                v_op("reduce_max", sgm2[:, :], smsk[:, :], AXX)
                v_drain()
                v_op("tensor_scalar", sge2[:, :], sgp[:, :], sgm2[:, :], None, ALU.subtract)
                v_drain()
                v_op("tensor_scalar", sge2[:, :], sge2[:, :], 1.0e30, 1.0, ALU.mult, ALU.add)
                v_drain()
                v_op("tensor_scalar", sge2[:, :], sge2[:, :], 1.0, 0.0, ALU.min, ALU.max)
                v_drain()
                v_op("tensor_tensor", scomb[:, th, :], sgp[:, :], sge2[:, :], ALU.mult, inc=True)
            comb_dv = dv.n
            dbg("comb", scomb[:, :, :].rearrange("p a b -> p (a b)"))
            pe_use(6)
            wt("t", dv, comb_dv)
            for th in range(2):
                t_tr(pb[6][0:8, th * 128:(th + 1) * 128], scomb[:, th, :], inc=(th == 1))
            wt("v", pe, pe.n)
            v_op("tensor_copy", scombT[0:8, 0:256], pb[6][0:8, 0:256], inc=True)
            combT_dv = dv.n
            bank_done(6, dv, combT_dv)

            for cc in range(DC):
                v_op("tensor_copy", xnp[:, cc, 0:T], sxn[:, cc * T:(cc + 1) * T])
                v_op("tensor_tensor", xnp[:, cc, T:2 * T], sxn[:, cc * T:(cc + 1) * T],
                     xnp[:, cc, 0:T], ALU.subtract, inc=(cc == DC - 1))
            xnp_dv = dv.n

            if _PHASES < 5 and m == 0:
                break
            for dc in range(DC):
                wt("v", pe, bc2_pe)
                v_op("tensor_tensor", stmp[:, :], sxn[:, dc * T:(dc + 1) * T], pb[4][:, 0:T], ALU.subtract)
                v_op("tensor_tensor", stmp2[:, :], stmp[:, :], pb[5][:, 0:T], ALU.mult)
                v_op("tensor_scalar", stmp[:, :], stmp2[:, :],
                     sgb[:, 2 * dc:2 * dc + 1], sgb[:, 2 * dc + 1:2 * dc + 2],
                     ALU.mult, ALU.add)
                v_op("tensor_copy", shp[:, dc * 2 * T:dc * 2 * T + T], stmp[:, :])
                v_op("tensor_tensor", shp[:, dc * 2 * T + T:(dc + 1) * 2 * T],
                     stmp[:, :], shp[:, dc * 2 * T:dc * 2 * T + T], ALU.subtract,
                     inc=True)
            h_dv = dv.n
            bank_done(4, dv, h_dv)
            bank_done(5, dv, h_dv)
            dbg("h", sh_[:, :])
            dbg("acc0", sacc[:, :])

            if _PHASES < 4 and m == 0:
                break
            lin_bank = {}
            for idx, oc in enumerate(ORDER_OC):
                b = idx % 4
                pe_use(b)
                wt("t", dv, h_dv)
                for dc in range(DC):
                    whi, wlo = next_wtile()
                    rhi = shp[:, dc * 2 * T:dc * 2 * T + T]
                    rlo = shp[:, dc * 2 * T + T:(dc + 1) * 2 * T]
                    t_mm(pb[b][:, 0:T], whi, rhi, dc == 0, False)
                    t_mm(pb[b][:, 0:T], wlo, rhi, False, False)
                    t_mm(pb[b][:, 0:T], whi, rlo, False, dc == DC - 1,
                         inc=pe if dc == DC - 1 else None)
                note_release()
                grp_pe = pe.n
                if oc <= 1:
                    wt("v", pe, grp_pe)
                    v_op("tensor_copy", sQ[:, oc * T:(oc + 1) * T], pb[b][:, 0:T], inc=True)
                    bank_done(b, dv, dv.n)
                elif oc <= 3:
                    wt("v", pe, grp_pe)
                    v_op("tensor_copy", sK[:, (oc - 2) * S + t0:(oc - 2) * S + t0 + T],
                         pb[b][:, 0:T], inc=True)
                    bank_done(b, dv, dv.n)
                elif oc < 36:
                    lin_bank[oc - 4] = (b, grp_pe)
                else:
                    j = oc - 36
                    lb, lin_pe = lin_bank.pop(j)
                    wt("s", pe, grp_pe)
                    wt("s", dv, gel_guard[0])
                    s_op("activation", sgel[:, :], pb[b][:, 0:T], AF.Gelu, inc=True)
                    gel_ac = ac.n
                    bank_done(b, ac, gel_ac)
                    wt("v", pe, lin_pe)
                    wt("v", ac, gel_ac)
                    if j < 16:
                        v_op("scalar_tensor_tensor", sgeg[:, :],
                             pb[lb][:, 0:T], 1.0, sgel[:, :], ALU.bypass, ALU.mult)
                        v_op("tensor_copy", glp[:, j * 2 * T:j * 2 * T + T], sgeg[:, :])
                        v_op("tensor_tensor", glp[:, j * 2 * T + T:(j + 1) * 2 * T],
                             sgeg[:, :], glp[:, j * 2 * T:j * 2 * T + T], ALU.subtract,
                             inc=True)
                        gel_guard[0] = dv.n
                        bank_done(lb, dv, dv.n)
                    else:
                        v_op("scalar_tensor_tensor", sgeg[:, :], pb[lb][:, 0:T], 1.0,
                             sgel[:, :], ALU.bypass, ALU.mult, inc=True)
                        geg_dv = dv.n
                        gel_guard[0] = dv.n
                        bank_done(lb, dv, geg_dv)
                        for half in range(2):
                            pidx = (2 * (j - 16) + half) % 4
                            pe_use(6)
                            wt("t", dv, vp_evac[pidx])
                            wt("t", dv, geg_dv)
                            t_tr(pb[6][:, pidx * 128:(pidx + 1) * 128],
                                 sgeg[:, half * 128:(half + 1) * 128], inc=True)
                            wt("v", pe, pe.n)
                            wt("v", vp_s[pidx], vpiece_gq[pidx])
                            v_op("tensor_copy", svp[:, pidx, :],
                                 pb[6][:, pidx * 128:(pidx + 1) * 128], inc=True)
                            vp_evac[pidx] = dv.n
                            wt("s", dv, dv.n)
                            s_dma(vsp[t0 + half * 128:t0 + (half + 1) * 128,
                                      (j - 16) * 128:(j - 15) * 128], svp[:, pidx, :], vp_s[pidx])
                            vpiece_gq[pidx] = vp_s[pidx].n
                        if j == 31:
                            vsp_ready[(m, 0)] = tuple(vv.n for vv in vp_s)
                            vsp_ready[(m, 1)] = tuple(vv.n for vv in vp_s)
            bank_done(6, dv, max(vp_evac))
            qk_dv = dv.n
            dbg("q", sQ[:, :])
            dbg("k", sK[:, :])
            dbg("gl", sgl[:, :])

            if _PHASES < 6 and m == 0:
                break
            for d_c in range(DC):
                b = d_c % 2
                pe_use(b)
                wt("t", dv, qk_dv)
                for e_c in range(16):
                    whi, wlo = next_wtile()
                    rhi = glp[:, e_c * 2 * T:e_c * 2 * T + T]
                    rlo = glp[:, e_c * 2 * T + T:(e_c + 1) * 2 * T]
                    t_mm(pb[b][:, 0:T], whi, rhi, e_c == 0, False)
                    t_mm(pb[b][:, 0:T], wlo, rhi, False, False)
                    t_mm(pb[b][:, 0:T], whi, rlo, False, e_c == 15,
                         inc=pe if e_c == 15 else None)
                note_release()
                wt("v", pe, pe.n)
                v_op("tensor_tensor", sacc[:, d_c * T:(d_c + 1) * T],
                     sacc[:, d_c * T:(d_c + 1) * T], pb[b][:, 0:T], ALU.add, inc=True)
                bank_done(b, dv, dv.n)

            if _PHASES < 7 and m == 0:
                break
            for pair in range(4):
                e0 = 2 * pair
                for ei in range(2):
                    pe_use(4 + ei)
                    wt("t", dv, combT_dv)
                    t_mm(pb[4 + ei][:, 0:T], ssel[:, (e0 + ei) * 128:(e0 + ei + 1) * 128],
                         scombT[:, 0:T], True, True, inc=pe)
                    wt("v", pe, pe.n)
                    v_op("tensor_copy", scb[:, ei, :], pb[4 + ei][:, 0:T], inc=True)
                    bank_done(4 + ei, dv, dv.n)
                cb_dv = dv.n
                for ei in range(2):
                    for ic in range(ICM):
                        pe_use(2)
                        pe_use(3)
                        wt("t", dv, xnp_dv)
                        for cc in range(DC):
                            whi, wlo = next_wtile()
                            t_mm(pb[2][:, 0:T], whi, xnp[:, cc, 0:T], cc == 0, False)
                            t_mm(pb[2][:, 0:T], wlo, xnp[:, cc, 0:T], False, False)
                            t_mm(pb[2][:, 0:T], whi, xnp[:, cc, T:2 * T], False,
                                 cc == DC - 1, inc=pe if cc == DC - 1 else None)
                        note_release()
                        h1_pe = pe.n
                        for cc in range(DC):
                            whi, wlo = next_wtile()
                            t_mm(pb[3][:, 0:T], whi, xnp[:, cc, 0:T], cc == 0, False)
                            t_mm(pb[3][:, 0:T], wlo, xnp[:, cc, 0:T], False, False)
                            t_mm(pb[3][:, 0:T], whi, xnp[:, cc, T:2 * T], False,
                                 cc == DC - 1, inc=pe if cc == DC - 1 else None)
                        note_release()
                        hh_pe = pe.n
                        wt("s", pe, h1_pe)
                        wt("s", dv, sil_guard[0])
                        s_op("activation", ssil[:, :], pb[2][:, 0:T], AF.Silu, inc=True)
                        sil_ac = ac.n
                        bank_done(2, ac, sil_ac)
                        wt("v", pe, hh_pe)
                        wt("v", ac, sil_ac)
                        wt("v", dv, cb_dv)
                        v_op("scalar_tensor_tensor", stmp[:, :], pb[3][:, 0:T], 1.0,
                             ssil[:, :], ALU.bypass, ALU.mult)
                        v_op("tensor_tensor", stmp2[:, :],
                             stmp[:, :], scb[:, ei, :], ALU.mult)
                        j2 = (ei * ICM + ic) * 2 * T
                        v_op("tensor_copy", scrp[:, j2:j2 + T], stmp2[:, :])
                        v_op("tensor_tensor", scrp[:, j2 + T:j2 + 2 * T], stmp2[:, :],
                             scrp[:, j2:j2 + T], ALU.subtract, inc=True)
                        sil_guard[0] = dv.n
                        bank_done(3, dv, dv.n)
                hpair_dv = dv.n
                for grp in range(2):
                    for cd in range(8):
                        pe_use(cd)
                    wt("t", dv, hpair_dv)
                    for ei in range(2):
                        for ic in range(ICM):
                            j2 = (ei * ICM + ic) * 2 * T
                            rhi = scrp[:, j2:j2 + T]
                            rlo = scrp[:, j2 + T:j2 + 2 * T]
                            first = ei == 0 and ic == 0
                            last = ei == 1 and ic == ICM - 1
                            for cd in range(8):
                                whi, wlo = next_wtile()
                                t_mm(pb[cd][:, 0:T], whi, rhi, first, False)
                                t_mm(pb[cd][:, 0:T], wlo, rhi, False, False)
                                t_mm(pb[cd][:, 0:T], whi, rlo, False, last,
                                     inc=pe if cd == 7 else None)
                            note_release()
                    wt("v", pe, pe.n)
                    for cd in range(8):
                        d_c = grp * 8 + cd
                        v_op("tensor_tensor", sacc[:, d_c * T:(d_c + 1) * T],
                             sacc[:, d_c * T:(d_c + 1) * T], pb[cd][:, 0:T], ALU.add,
                             inc=(cd == 7))
                    for cd in range(8):
                        bank_done(cd, dv, dv.n)

            if _PHASES < 8 and m == 0:
                break
            for ic in range(ICS):
                pe_use(2)
                pe_use(3)
                wt("t", dv, xnp_dv)
                for cc in range(DC):
                    whi, wlo = next_wtile()
                    t_mm(pb[2][:, 0:T], whi, xnp[:, cc, 0:T], cc == 0, False)
                    t_mm(pb[2][:, 0:T], wlo, xnp[:, cc, 0:T], False, False)
                    t_mm(pb[2][:, 0:T], whi, xnp[:, cc, T:2 * T], False,
                         cc == DC - 1, inc=pe if cc == DC - 1 else None)
                note_release()
                s1_pe = pe.n
                for cc in range(DC):
                    whi, wlo = next_wtile()
                    t_mm(pb[3][:, 0:T], whi, xnp[:, cc, 0:T], cc == 0, False)
                    t_mm(pb[3][:, 0:T], wlo, xnp[:, cc, 0:T], False, False)
                    t_mm(pb[3][:, 0:T], whi, xnp[:, cc, T:2 * T], False,
                         cc == DC - 1, inc=pe if cc == DC - 1 else None)
                note_release()
                ss_pe = pe.n
                wt("s", pe, s1_pe)
                wt("s", dv, sil_guard[0])
                s_op("activation", ssil[:, :], pb[2][:, 0:T], AF.Silu, inc=True)
                sil_ac = ac.n
                bank_done(2, ac, sil_ac)
                wt("v", pe, ss_pe)
                wt("v", ac, sil_ac)
                v_op("scalar_tensor_tensor", stmp2[:, :], pb[3][:, 0:T],
                     1.0, ssil[:, :], ALU.bypass, ALU.mult)
                j2 = ic * 2 * T
                v_op("tensor_copy", scrp[:, j2:j2 + T], stmp2[:, :])
                v_op("tensor_tensor", scrp[:, j2 + T:j2 + 2 * T], stmp2[:, :],
                     scrp[:, j2:j2 + T], ALU.subtract, inc=True)
                sil_guard[0] = dv.n
                bank_done(3, dv, dv.n)
            s_dv = dv.n
            dbg("scr", scr[:, :])
            for grp in range(2):
                for cd in range(8):
                    pe_use(cd)
                wt("t", dv, s_dv)
                for ic in range(ICS):
                    j2 = ic * 2 * T
                    rhi = scrp[:, j2:j2 + T]
                    rlo = scrp[:, j2 + T:j2 + 2 * T]
                    for cd in range(8):
                        whi, wlo = next_wtile()
                        t_mm(pb[cd][:, 0:T], whi, rhi, ic == 0, False)
                        t_mm(pb[cd][:, 0:T], wlo, rhi, False, False)
                        t_mm(pb[cd][:, 0:T], whi, rlo, False, ic == ICS - 1,
                             inc=pe if cd == 7 else None)
                    note_release()
                wt("v", pe, pe.n)
                for cd in range(8):
                    d_c = grp * 8 + cd
                    v_op("tensor_tensor", sacc[:, d_c * T:(d_c + 1) * T],
                         sacc[:, d_c * T:(d_c + 1) * T], pb[cd][:, 0:T], ALU.add,
                         inc=(cd == 7))
                for cd in range(8):
                    bank_done(cd, dv, dv.n)
            dbg("accA", sacc[:, :])

            if _PHASES < 9 and m == 0:
                break
            for qc in range(2):
                qb = t0 + qc * 128
                kend = qb + 128
                v_op("memset", sacct[:, :], 0.0)
                v_op("memset", sl_[:, :], 0.0)
                v_op("memset", sm_[:, :], NEG, inc=True)
                k0 = 0
                while k0 < kend:
                    kw = min(512, kend - k0)
                    s_rel = k0 - qb
                    if s_rel <= -512:
                        var = 4
                        cs_imm = float(s_rel)
                    else:
                        var = {0: 0, -128: 1, -256: 2, -384: 3}[s_rel]
                        cs_imm = 0.0
                    nkc = kw // 128
                    pe_use(4)
                    wt("t", dv, qk_dv)
                    for d in range(QKC):
                        t_mm(pb[4][:, 0:kw],
                             sQ[:, d * T + qc * 128:d * T + (qc + 1) * 128],
                             sK[:, d * S + k0:d * S + k0 + kw],
                             d == 0, d == QKC - 1, inc=pe if d == QKC - 1 else None)
                    sc_pe = pe.n
                    kt_xq = []
                    for kc in range(nkc):
                        slot = ((k0 // 128) + kc) % 2
                        rb = (k0 + kc * 128) // 128
                        src_m = (rb * 128) // T
                        half = ((rb * 128) % T) // 128
                        if _BLV:
                            wt("s", ppo, 16 * NSPLIT)
                            for vi_, vv_ in enumerate(vsp_ready[(src_m, half)]):
                                wt("s", vp_s[vi_], vv_)
                            wt("s", pe, vin_rel[slot])
                            s_dma(svin[:, slot, :], vsp[rb * 128:(rb + 1) * 128, :], vi_s[slot])
                        kt_xq.append((slot, vi_s[slot].n))
                    wt("v", pe, sc_pe)
                    v_op("tensor_scalar", scs[:, :], ssp[:, :], cs_imm, None, ALU.mult)
                    v_op("scalar_tensor_tensor", sX[:, 0:kw], pb[4][:, 0:kw], SCALE,
                         sB[:, var * 512:var * 512 + kw], ALU.mult, ALU.add, inc=True)
                    bank_done(4, dv, dv.n)
                    v_op("reduce_max", smt[:, :], sX[:, 0:kw], AXX)
                    v_drain()
                    v_op("tensor_tensor", smt[:, :], smt[:, :], scs[:, :], ALU.add)
                    v_drain()
                    v_op("tensor_tensor", smn[:, :], sm_[:, :], smt[:, :], ALU.max)
                    v_drain()
                    v_op("tensor_tensor", sda[:, :], sm_[:, :], smn[:, :], ALU.subtract)
                    v_op("tensor_tensor", sbv[:, :], scs[:, :], smn[:, :], ALU.subtract)
                    v_drain()
                    v_op("tensor_copy", sm_[:, :], smn[:, :], inc=True)
                    pre_ac_dv = dv.n
                    if _BLIM < 2:
                        k0 += kw
                        continue
                    wt("s", dv, pre_ac_dv)
                    wt("s", pe, prev_tr_pe)
                    wt("s", dv, prev_resc_dv)
                    s_op("activation", salpha[:, :], sda[:, :], AF.Exp, inc=True)
                    s_op("activation", sP[:, 0:kw], sX[:, 0:kw], AF.Exp, bias=sbv[:, :],
                         accum_out=srow[:, :], inc=True)
                    p_ac = ac.n
                    wt("v", ac, p_ac)
                    v_op("tensor_scalar", sl_[:, :], sl_[:, :], salpha[:, :], srow[:, :],
                         ALU.mult, ALU.add)
                    v_op("tensor_scalar", sacct[:, :], sacct[:, :], salpha[:, :], None,
                         ALU.mult, inc=True)
                    resc_dv = dv.n
                    prev_resc_dv = resc_dv
                    if _BLIM < 3:
                        k0 += kw
                        continue
                    pe_use(5)
                    wt("t", ac, p_ac)
                    for kc in range(nkc):
                        t_tr(pb[5][:, kc * 128:(kc + 1) * 128],
                             sP[:, kc * 128:(kc + 1) * 128], inc=(kc == nkc - 1))
                    prev_tr_pe = pe.n
                    wt("v", pe, prev_tr_pe)
                    v_op("tensor_copy", sPt[:, 0:kw], pb[5][:, 0:kw], inc=True)
                    pt_dv = dv.n
                    bank_done(5, dv, pt_dv)
                    if _BLIM < 4:
                        k0 += kw
                        continue
                    for vd in range(4):
                        pe_use(vd)
                    wt("t", dv, pt_dv)
                    for kc in range(nkc):
                        slot, xv = kt_xq[kc]
                        wt("t", vi_s[slot], xv)
                        for vd in range(4):
                            t_mm(pb[vd][:, 0:512], sPt[:, kc * 128:(kc + 1) * 128],
                                 svin[:, slot, vd * 512:(vd + 1) * 512],
                                 kc == 0, kc == nkc - 1, inc=pe if vd == 3 else None)
                        vin_rel[slot] = pe.n
                    if _BLIM < 5:
                        k0 += kw
                        continue
                    wt("v", pe, pe.n)
                    wt("v", dv, resc_dv)
                    for vd in range(4):
                        v_op("tensor_tensor", sacct[:, vd * 512:(vd + 1) * 512],
                             sacct[:, vd * 512:(vd + 1) * 512], pb[vd][:, 0:512],
                             ALU.add, inc=(vd == 3))
                    for vd in range(4):
                        bank_done(vd, dv, dv.n)
                    k0 += kw
                if _BLIM < 6:
                    continue
                v_drain()
                v_op("reciprocal", slinv[:, :], sl_[:, :])
                v_drain()
                v_op("tensor_scalar", sacct[:, :], sacct[:, :], slinv[:, :], None,
                     ALU.mult, inc=True)
                fin_dv = dv.n
                if qc == 0:
                    dbg("acct", sacct[:, :])
                for vq in range(4):
                    pe_use(6)
                    wt("t", dv, fin_dv)
                    for jj in range(4):
                        t_tr(pb[6][:, jj * 128:(jj + 1) * 128],
                             sacct[:, (vq * 4 + jj) * 128:(vq * 4 + jj + 1) * 128],
                             inc=(jj == 3))
                    wt("v", pe, pe.n)
                    v_op("tensor_copy", safm[:, vq * 512:(vq + 1) * 512],
                         pb[6][:, 0:512], inc=True)
                    bank_done(6, dv, dv.n)
                afm_dv = dv.n
                if qc == 0:
                    dbg("afm", safm[:, :])
                if _BLIM < 7:
                    continue
                for d_c in range(DC):
                    b = d_c % 2
                    pe_use(b)
                    wt("t", dv, afm_dv)
                    for e_c in range(16):
                        ap, _none = next_wtile()
                        t_mm(pb[b][:, 0:128], ap, safm[:, e_c * 128:(e_c + 1) * 128],
                             e_c == 0, e_c == 15, inc=pe if e_c == 15 else None)
                    note_release()
                    wt("v", pe, pe.n)
                    v_op("tensor_tensor",
                         sacc[:, d_c * T + qc * 128:d_c * T + (qc + 1) * 128],
                         sacc[:, d_c * T + qc * 128:d_c * T + (qc + 1) * 128],
                         pb[b][:, 0:128], ALU.add, inc=True)
                    bank_done(b, dv, dv.n)

            wt("g", dv, dv.n)
            yv = y.rearrange("(c p) t -> p c t", p=128)
            g_dma(yv[:, :, t0:t0 + T], sacc[:, :].rearrange("p (c t) -> p c t", c=DC), yq)
            y_guard = yq.n

            em("t", lambda h: h.drain())
            pe.n += 1
            em("t", lambda h: h.nop(nofuse=True).then_inc(pe.h, 1))
            em("v", lambda h: h.drain())
            _trk["v"][0].clear()
            _trk["v"][1].clear()
            dv.n += 1
            em("v", lambda h: h.nop(nofuse=True).then_inc(dv.h, 1))
            wt("s", pe, pe.n)
            wt("s", dv, dv.n)
            em("s", lambda h: h.drain())
            _trk["s"][0].clear()
            _trk["s"][1].clear()
            ac.n += 1
            em("s", lambda h: h.nop(nofuse=True).then_inc(ac.h, 1))
            wt("t", dv, dv.n - 1)
            wt("t", ac, ac.n)
            wt("v", pe, pe.n)
            wt("v", ac, ac.n)

        wt("g", yq, yq.n)
        for vv in vp_s:
            wt("g", vv, vv.n)
        wt("g", dq, dq.n)
        em("g", lambda h: h.nop())

        if _PHASES >= 99 and _BLIM >= 99:
            assert wstate["pos"] == TILES_PER_MACRO * _NMACRO, wstate["pos"]

        @block.tensor
        def _(h):
            for f in OPS["t"]:
                f(h)

        @block.vector
        def _(h):
            for f in OPS["v"]:
                f(h)

        @block.scalar
        def _(h):
            for f in OPS["s"]:
                f(h)

        @block.sync
        def _(h):
            for f in OPS["y"]:
                f(h)

        @block.gpsimd
        def _(h):
            for f in OPS["g"]:
                f(h)

    return nc


def _pack_weights(expand, project, w1, w2, w3, sw1, sw2, sw3):
    tiles = []

    def tl(matT, cc, oc):
        return matT[cc * 128:(cc + 1) * 128, oc * 128:(oc + 1) * 128]

    eT = np.ascontiguousarray(expand.T)
    for oc in ORDER_OC:
        for dc in range(16):
            tiles.append(tl(eT, dc, oc))
    p1T = np.ascontiguousarray(project[:, :2048].T)
    for d_c in range(16):
        for e_c in range(16):
            tiles.append(tl(p1T, e_c, d_c))
    for pair in range(4):
        es = (2 * pair, 2 * pair + 1)
        for e in es:
            w1T = np.ascontiguousarray(w1[e].T)
            w2T = np.ascontiguousarray(w2[e].T)
            for ic in range(ICM):
                for cc in range(16):
                    tiles.append(tl(w1T, cc, ic))
                for cc in range(16):
                    tiles.append(tl(w2T, cc, ic))
        for grp in range(2):
            for e in es:
                w3T = np.ascontiguousarray(w3[e].T)
                for ic in range(ICM):
                    for cd in range(8):
                        tiles.append(tl(w3T, ic, grp * 8 + cd))
    s1T = np.ascontiguousarray(sw1.T)
    s2T = np.ascontiguousarray(sw2.T)
    for ic in range(ICS):
        for cc in range(16):
            tiles.append(tl(s1T, cc, ic))
        for cc in range(16):
            tiles.append(tl(s2T, cc, ic))
    s3T = np.ascontiguousarray(sw3.T)
    for grp in range(2):
        for ic in range(ICS):
            for cd in range(8):
                tiles.append(tl(s3T, ic, grp * 8 + cd))
    p2T = np.ascontiguousarray(project[:, 2048:].T)
    for _qc in range(2):
        for d_c in range(16):
            for e_c in range(16):
                tiles.append(tl(p2T, e_c, d_c))
    assert len(tiles) == TILES_PER_MACRO, len(tiles)
    arr = np.stack(tiles)
    return np.ascontiguousarray(arr.transpose(1, 0, 2).reshape(128, -1))


def kernel(L0_attn_x, L0_attn_ln_gamma, L0_attn_ln_beta, L0_attn_expand,
           L0_attn_project, L0_attn_pos_mult, L0_ffn_gate_w, L0_ffn_w1,
           L0_ffn_w2, L0_ffn_w3, L0_ffn_sw1, L0_ffn_sw2, L0_ffn_sw3):
    xs = np.asarray(L0_attn_x, dtype=np.float32)
    wpack = _pack_weights(np.asarray(L0_attn_expand, np.float32),
                          np.asarray(L0_attn_project, np.float32),
                          np.asarray(L0_ffn_w1, np.float32),
                          np.asarray(L0_ffn_w2, np.float32),
                          np.asarray(L0_ffn_w3, np.float32),
                          np.asarray(L0_ffn_sw1, np.float32),
                          np.asarray(L0_ffn_sw2, np.float32),
                          np.asarray(L0_ffn_sw3, np.float32))
    gate_w = np.asarray(L0_ffn_gate_w, np.float32)
    gwt = np.ascontiguousarray(gate_w.T.reshape(16, 128, 8).transpose(1, 0, 2).reshape(128, 128))
    gamma = np.asarray(L0_attn_ln_gamma, np.float32).reshape(16, 128).T
    beta = np.asarray(L0_attn_ln_beta, np.float32).reshape(16, 128).T
    gbp = np.ascontiguousarray(np.stack([gamma, beta], axis=-1).reshape(128, 32))
    pmv = np.ascontiguousarray(np.broadcast_to(
        np.asarray(L0_attn_pos_mult, np.float32).reshape(()), (128, 1)))

    nc = build_nc()
    in_maps = []
    for i in range(8):
        in_maps.append({
            "x": np.ascontiguousarray(xs[i].T),
            "w": wpack,
            "gwt": gwt,
            "gb": gbp,
            "pm": pmv,
        })
    res = run_bass_kernel_spmd(nc, in_maps, core_ids=list(range(8)))
    out = np.stack([np.asarray(res.results[i]["y"]).T for i in range(8)])
    return np.ascontiguousarray(out.astype(np.float32))
